# version 01
# baseline (speedup 1.0000x reference)
import numpy as np
import concourse.bass as bass
import concourse.mybir as mybir

F32 = mybir.dt.float32
BF16 = mybir.dt.bfloat16
AF = mybir.ActivationFunctionType
ALU = mybir.AluOpType
AX = mybir.AxisListType

NTOK = 4096
DM = 2048
H = 16
D = 128
FF = 8192
TT = 256
NT1 = NTOK // TT
NSB = NTOK // 128
POW45 = float(2.0 ** 45)

ENGS = ("sync", "gpsimd", "vector", "scalar", "tensor")


class Prog:
    def __init__(self, nc, ctx):
        self.nc = nc
        self.ctx = ctx
        self.q = {e: [] for e in ENGS}
        self.sems = {}
        self.cnt = {}
        self.lastw = {}
        self._bar = []

    def sem(self, name):
        if name not in self.sems:
            self.sems[name] = self.ctx.enter_context(self.nc.semaphore(name))
            self.cnt[name] = 0
        return self.sems[name]

    def tok(self, name):
        return (name, self.cnt[name])

    def step(self, eng, waits, emit, sem=None, dma=False):
        waits = [(s, v) for (s, v) in list(waits) + self._bar if v > 0]
        eff = []
        for s, v in waits:
            k = (eng, s)
            if self.lastw.get(k, 0) < v:
                eff.append((s, v))
                self.lastw[k] = v
        if sem is not None:
            sh = self.sem(sem)

        def run(e, eff=tuple(eff), emit=emit, sem=sem, dma=dma):
            for s, v in eff:
                e.wait_ge(self.sems[s], v)
            last = emit(e)
            if sem is not None:
                if dma:
                    insts = last if isinstance(last, (list, tuple)) else [last]
                    for ins in insts:
                        ins.then_inc(self.sems[sem], 16)
                else:
                    if isinstance(last, (list, tuple)):
                        last = last[-1]
                    if last is None:
                        last = e.nop()
                    last.then_inc(self.sems[sem], 1)
        self.q[eng].append(run)
        if sem is not None:
            if dma:
                n = getattr(emit, "ndma", 1)
                self.cnt[sem] += 16 * n
            else:
                self.cnt[sem] += 1
            return (sem, self.cnt[sem])
        return None

    def barrier(self):
        self._bar = [(s, self.cnt[s]) for s in self.sems]

    def wait_only(self, eng, waits):
        self.step(eng, waits, lambda e: None, sem=None)

    def finalize(self):
        nc = self.nc
        with nc.Block() as block:
            @block.sync
            def _(e):
                for f in self.q["sync"]:
                    f(e)

            @block.gpsimd
            def _(e):
                for f in self.q["gpsimd"]:
                    f(e)

            @block.vector
            def _(e):
                for f in self.q["vector"]:
                    f(e)

            @block.scalar
            def _(e):
                for f in self.q["scalar"]:
                    f(e)

            @block.tensor
            def _(e):
                for f in self.q["tensor"]:
                    f(e)


def dma_step(p, eng, sem, waits, pairs):
    def emit(e):
        return [e.dma_start(out=o, in_=i) for (o, i) in pairs]
    emit.ndma = len(pairs)
    return p.step(eng, waits, emit, sem=sem, dma=True)



def ts2(e, out, in0, s1, s2, op0, op1):
    return e.tensor_scalar(out=out, in0=in0, scalar1=s1, scalar2=s2,
                           op0=op0, op1=op1)



def ln_stats(p, x_ap, n, dump, eps_ap, negmu, inv, sc, waits, tag):
    import os
    if os.environ.get("VSQ"):
        tA = p.step("vector", waits,
                    lambda e: e.tensor_tensor(out=dump, in0=x_ap, in1=x_ap,
                                              op=ALU.mult),
                    sem="cV")
    else:
        tA = p.step("scalar", waits,
                    lambda e: e.activation(out=dump, in_=x_ap, func=AF.Square),
                    sem="cA")
    tV = p.step("vector", waits,
                lambda e: e.reduce_sum(out=sc["ssum"][:, :], in_=x_ap, axis=AX.X),
                sem="cV")
    tV = p.step("vector", [tA, tV],
                lambda e: e.reduce_sum(out=sc["ssq"][:, :], in_=dump, axis=AX.X),
                sem="cV")

    def comb1(e):
        e.tensor_scalar(out=negmu, in0=sc["ssum"][:, :], scalar1=-1.0 / n,
                        scalar2=None, op0=ALU.mult)
        return e.tensor_scalar(out=sc["v1"][:, :], in0=sc["ssq"][:, :],
                               scalar1=1.0 / n, scalar2=None, op0=ALU.mult)
    tC1 = p.step("vector", [tA, tV], comb1, sem="cV")
    tC2 = p.step("vector", [tC1],
                 lambda e: e.tensor_tensor(out=sc["v2"][:, :], in0=negmu,
                                           in1=negmu, op=ALU.mult),
                 sem="cV")
    tC = p.step("vector", [tC2],
                lambda e: e.tensor_tensor(out=sc["v1"][:, :], in0=sc["v1"][:, :],
                                          in1=sc["v2"][:, :], op=ALU.subtract),
                sem="cV")
    tS = p.step("scalar", [tC],
                lambda e: e.activation(out=sc["v2"][:, :], in_=sc["v1"][:, :],
                                       func=AF.Sqrt, bias=eps_ap),
                sem="cA")
    tI = p.step("vector", [tS],
                lambda e: e.reciprocal(out=inv, in_=sc["v2"][:, :]),
                sem="cV")
    return tI



def ladder(p, z, m, n, waits, tag):
    t = None
    for k in range(7, -1, -1):
        w = waits if t is None else [t]
        thr = float(2.0 ** k) - 0.5
        sc = -float(2.0 ** k)
        tM = p.step("vector", w,
                    lambda e, thr=thr, sc=sc: ts2(e, m, z, thr, sc,
                                                  ALU.is_ge, ALU.mult),
                    sem="cV")
        t = p.step("vector", [tM],
                   lambda e: e.tensor_tensor(out=z, in0=z, in1=m, op=ALU.add),
                   sem="cV")
    return t



class Overlay:
    def __init__(self, nc, base_handle):
        self.nc = nc
        self.base = nc.lookup_mloc(base_handle).addr
        self.cur = 0

    def mk(self, name, shape, dtype, at=None):
        import functools
        import operator
        nc = self.nc
        h = nc.virtual_sbuf_tensor(name, shape, dtype)
        nc._tensor(h.name, h.shape, h.dtype, type="SB")
        mloc = nc.lookup_mloc(h)
        mloc.allocated = not nc.target_bir_lowering
        per_part = functools.reduce(operator.mul, shape[1:]) * mybir.dt.size(dtype)
        if not hasattr(self, "addrs"):
            self.addrs = {}
        if at is not None:
            mloc.addr = self.addrs[at]
        else:
            mloc.addr = self.base + self.cur
            self.cur += (per_part + 31) // 32 * 32
            assert self.base + self.cur <= 229376 - 2048, (
                f"overlay overflow at {name}: {self.base + self.cur}")
        self.addrs[name] = mloc.addr
        return h




def ladder2(p, z, m, s45, b45, waits, tag):
    t = None
    for k in range(7, -1, -1):
        w = waits if t is None else [t]
        tM = p.step("scalar", w,
                    lambda e, k=k: e.activation(out=m, in_=z, func=AF.Sigmoid,
                                                scale=s45[:, :],
                                                bias=b45[:, 7 - k:8 - k]),
                    sem="cA")
        t = p.step("vector", [tM],
                   lambda e, k=k: e.scalar_tensor_tensor(
                       out=z, in0=m, scalar=-float(2.0 ** k), in1=z,
                       op0=ALU.mult, op1=ALU.add),
                   sem="cV")
    return t


def p1_build(p, io):
    nc = p.nc
    ctx = p.ctx
    sb = lambda n, s, d=F32: ctx.enter_context(nc.sbuf_tensor(n, s, d))

    WQO = sb("p1_wqo", [128, H * H * 128], BF16)
    QHB = sb("p1_qhb", [128, H * DM], BF16)
    L1T = sb("p1_l1t", [128, 16 * 512], BF16)
    XR = sb("p1_xr", [128, 2 * DM], F32)
    Z0 = sb("p1_z0", [128, DM], F32)
    RQT = sb("p1_rqt", [128, 16 * 128], BF16)
    SCTH = sb("p1_scth", [16, DM], BF16)
    ov = Overlay(nc, L1T)
    OF = ov.mk("p1_of", [128, DM], BF16)
    GT0 = ov.mk("p1_gt0", [128, 512], F32)
    GT = ov.mk("p1_gtt", [128, 512], BF16)
    O2 = ov.mk("p1_o2", [128, 512], BF16)
    TB2 = ov.mk("p1_tb2", [128, 512], BF16)
    TB1 = ov.mk("p1_tb1", [128, 512], F32)
    SST = ov.mk("p1_sst", [4, 1024], BF16)
    assert ov.cur <= 16 * 512 * 2, f"L1T overlay overflow {ov.cur}"
    Y = ov.mk("p1_Y", [128, DM], F32, at="p1_of")
    X1 = ov.mk("p1_X1", [128, DM], F32, at="p1_tb2")
    ovz = Overlay(nc, Z0)
    SCTB = ovz.mk("p1_sctb", [16, DM], BF16)
    W0T = sb("p1_w0t", [128, H * 128], BF16)
    W1T = sb("p1_w1t", [128, H * 128], BF16)
    W2T = sb("p1_w2t", [128, H * 128], BF16)
    WsT = sb("p1_ws", [128, 16 * 16], BF16)
    RS1 = sb("p1_rs1", [16, H * 128], BF16)
    RSB4 = sb("p1_rsb4", [4, 4 * 128], BF16)
    Zb = sb("p1_zb", [128, 31], BF16)
    ONES4 = sb("p1_ones4", [128, 16], BF16)
    LNG = sb("p1_lng", [128, 1], F32)
    LNB = sb("p1_lnb", [128, 1], F32)
    EYE = sb("p1_eye", [128, 128], F32)
    BSV = sb("p1_bs", [16, 1], F32)
    EPS5 = sb("p1_eps5", [128, 1], F32)
    EPS6 = sb("p1_eps6", [128, 1], F32)
    EPSQ = sb("p1_epsq", [16, 1], F32)
    S45L = sb("p1_s45", [128, 1], F32)
    B45L = sb("p1_b45", [128, 8], F32)
    M1C = sb("p1_m1c", [128, 16], F32)
    SQ = sb("p1_sq", [16, 512], F32)
    SA = sb("p1_sa", [16, 512], F32)
    IVQ = sb("p1_ivq", [16, 512], BF16)
    SSTA = sb("p1_ssta", [4, 512], F32)
    SSTB = sb("p1_sstb", [4, 512], F32)
    Q2T = sb("p1_q2t", [128, 512], BF16)
    st = {k: sb("p1_st_" + k, [128, 1], F32)
          for k in ["ssum", "ssq", "v1", "v2", "negmu", "inv", "m1", "qs"]}
    psA = ctx.enter_context(nc.psum_tensor("p1_psA", [128, DM], F32))
    psB = ctx.enter_context(nc.psum_tensor("p1_psB", [128, DM], F32))

    def init_consts(e):
        e.memset(Zb[:, 0:15], 0.0)
        e.memset(Zb[:, 15:16], 1.0)
        e.memset(Zb[:, 16:31], 0.0)
        e.memset(ONES4[:, :], 0.0)
        for mu in range(4):
            e.memset(ONES4[:, mu * 5:mu * 5 + 1], 1.0)
        e.memset(EPS5[:], 1e-5)
        e.memset(EPS6[:], 1e-6)
        e.memset(EPSQ[:], 0.0)
        e.memset(S45L[:], POW45)
        for k in range(8):
            e.memset(B45L[:, 7 - k:8 - k], -(2.0 ** k - 0.5) * POW45)
        return e.memset(st["ssum"][:, :], 0.0)
    tCon = p.step("vector", [], init_consts, sem="cV")

    def ld_wq(e):
        ins = []
        for h in range(H):
            for kc in range(16):
                ins.append(e.dma_start(
                    out=WQO[:, (kc * 16 + h) * 128:(kc * 16 + h + 1) * 128],
                    in_=io["Wqkv"][kc * 128:(kc + 1) * 128,
                                   h * 384:h * 384 + 128]))
        for kc in range(16):
            ins.append(e.dma_start(
                out=WsT[:, kc * 16:(kc + 1) * 16],
                in_=io["Ws"][kc * 128:(kc + 1) * 128, :]))
        return ins
    ld_wq.ndma = H * 16 + 16
    import os
    ld_wq_fn = ld_wq
    if os.environ.get("SKIPW"):
        ld_wq_fn = lambda e: [e.dma_start(out=WQO[:, 0:128],
                                          in_=io["Wqkv"][0:128, 0:128])]
        ld_wq_fn.ndma = 1
    tW = p.step("gpsimd", [tCon], ld_wq_fn, sem="dG", dma=True)

    def ld_small(e):
        ins = [e.dma_start(out=RS1[:, :], in_=io["rs1"][:, :]),
               e.dma_start(out=RSB4[:, :], in_=io["rsb4"][:, :]),
               e.dma_start(out=LNG[:, 0:1], in_=io["lng_c"][:, :]),
               e.dma_start(out=LNB[:, 0:1], in_=io["lnb_c"][:, :]),
               e.dma_start(out=EYE[:], in_=io["eye"][:, :]),
               e.dma_start(out=BSV[:, 0:1], in_=io["bs_c"][:, :])]
        return ins
    ld_small.ndma = 6
    tWs = p.step("sync", [], ld_small, sem="dS", dma=True)

    evac_tok = None
    wlist = [] if os.environ.get("SKIPW") else [(W0T, "w0"), (W1T, "w1"),
                                                (W2T, "w2")]
    XT0 = XR[:, 0:DM]
    for i, (WT, key) in enumerate(wlist):
        def ldw(e, key=key):
            src_ap = io[key][:, :, :].rearrange("h q c -> q h c")
            return [e.dma_start(out=XT0, in_=src_ap)]
        ldw.ndma = 1
        wts = [tWs, tCon] + ([evac_tok] if evac_tok else [])
        tLd = p.step("sync", wts, ldw, sem="dS", dma=True)
        for bi in range(4):
            def trw(e, bi=bi):
                last = None
                for jj in range(4):
                    hh = bi * 4 + jj
                    last = e.transpose(psA[:, jj * 512:jj * 512 + 128],
                                       XT0[:, hh * 128:(hh + 1) * 128], EYE[:])
                return last
            wts = [tLd] + ([evac_tok] if evac_tok else [])
            tT = p.step("tensor", wts, trw, sem="cT")

            def evw(e, bi=bi, WT=WT):
                last = None
                for jj in range(4):
                    hh = bi * 4 + jj
                    last = e.tensor_copy(
                        out=WT[:, hh * 128:(hh + 1) * 128],
                        in_=psA[:, jj * 512:jj * 512 + 128])
                return last
            evac_tok = p.step("vector", [tT], evw, sem="cV")
    tWt = evac_tok if evac_tok is not None else tCon
    return dict(WQO=WQO, QHB=QHB, L1T=L1T, XR=XR, Z0=Z0, RQT=RQT, SCTB=SCTB,
                SCTH=SCTH, OF=OF, GT0=GT0, GT=GT, O2=O2, TB1=TB1, TB2=TB2,
                SST=SST, SSTA=SSTA, SSTB=SSTB, Y=Y, X1=X1, W0T=W0T, W1T=W1T, W2T=W2T,
                WsT=WsT, RS1=RS1, RSB4=RSB4, Zb=Zb, ONES4=ONES4, LNG=LNG,
                LNB=LNB, EYE=EYE, BSV=BSV, EPS5=EPS5, EPS6=EPS6, EPSQ=EPSQ,
                S45L=S45L, B45L=B45L, M1C=M1C, SQ=SQ, SA=SA, IVQ=IVQ, Q2T=Q2T,
                st=st, psA=psA, psB=psB, tInit=[tWt, tWs, tCon, tW], tWq=tW)


def dbg_store(p, io, name, tok, ap):
    if io.get("dbg") and name in io["dbg"]:
        return dma_step(p, "gpsimd", "dG", [tok], [(io["dbg"][name][:, :], ap)])
    return None


def p1_tiles(p, b, io):
    import os
    STOPAT = os.environ.get("STOPAT", "")
    WQO, QHB, L1T, XR, Z0 = b["WQO"], b["QHB"], b["L1T"], b["XR"], b["Z0"]
    RQT, SCTB, SCTH = b["RQT"], b["SCTB"], b["SCTH"]
    OF, GT0, GT, O2, TB1, TB2 = b["OF"], b["GT0"], b["GT"], b["O2"], b["TB1"], b["TB2"]
    SST, SSTA, SSTB, Y, X1 = b["SST"], b["SSTA"], b["SSTB"], b["Y"], b["X1"]
    W0T, W1T, W2T, WsT = b["W0T"], b["W1T"], b["W2T"], b["WsT"]
    RS1, RSB4, Zb, ONES4 = b["RS1"], b["RSB4"], b["Zb"], b["ONES4"]
    LNG, LNB, EYE, BSV = b["LNG"], b["LNB"], b["EYE"], b["BSV"]
    EPS5, EPS6, EPSQ = b["EPS5"], b["EPS6"], b["EPSQ"]
    S45L, B45L, M1C = b["S45L"], b["B45L"], b["M1C"]
    SQ, SA, IVQ, Q2T = b["SQ"], b["SA"], b["IVQ"], b["Q2T"]
    st = b["st"]
    psA, psB = b["psA"], b["psB"]
    XRH = [XR[:, 0:DM], XR[:, DM:2 * DM]]
    QHB3 = QHB[:, :].rearrange("p (h t) -> p h t", t=DM)
    wq_tok = b["tWq"]
    init_toks = b["tInit"]
    last_store = None

    for bb in range(2):
        p.barrier()
        if bb == 1:
            def ld_wq2(e):
                ins = []
                for h in range(H):
                    for kc in range(16):
                        ins.append(e.dma_start(
                            out=WQO[:, (kc * 16 + h) * 128:(kc * 16 + h + 1) * 128],
                            in_=io["Wqkv"][kc * 128:(kc + 1) * 128,
                                           h * 384:h * 384 + 128]))
                return ins
            ld_wq2.ndma = H * 16
            if os.environ.get("SKIPW"):
                ld_wq2 = lambda e: [e.dma_start(out=WQO[:, 0:128],
                                                in_=io["Wqkv"][0:128, 0:128])]
                ld_wq2.ndma = 1
            wq_tok = p.step("gpsimd", [], ld_wq2, sem="dG", dma=True)

        xr_free = [None, None]
        st_free = None
        psA_free = init_toks[0] if bb == 0 else None
        l1t_mm = None
        sct_evac = None
        sil_free = [None, None]
        ivq_read = None
        for s in range(4):
            ev_last = None
            for g in range(4):
                gg = s * 4 + g
                half = gg % 2
                xr = XRH[half]
                w = ([init_toks[0]] if (bb == 0 and gg == 0) else []) + \
                    ([xr_free[half]] if xr_free[half] else [])
                tX = dma_step(p, "sync", "dS", w,
                              [(xr, io["x"][bb * 2048 + gg * 128:
                                            bb * 2048 + (gg + 1) * 128, :])])
                wq2 = [tX] + ([psA_free] if psA_free else []) + \
                    ([st_free] if st_free else [])
                tDmp = p.step("scalar", wq2,
                              lambda e, xr=xr: e.activation(
                                  out=psA[:, :], in_=xr, func=AF.Square),
                              sem="cA")
                tQ = p.step("vector", [tDmp] + ([st_free] if st_free else []),
                            lambda e: e.reduce_sum(out=st["ssq"][:, :],
                                                   in_=psA[:, :], axis=AX.X),
                            sem="cV")
                tS = p.step("vector", [tX] + ([st_free] if st_free else []),
                            lambda e, xr=xr: e.reduce_sum(
                                out=st["ssum"][:, :], in_=xr, axis=AX.X),
                            sem="cV")

                def comb1(e):
                    e.tensor_scalar(out=st["negmu"][:, :], in0=st["ssum"][:, :],
                                    scalar1=-1.0 / DM, scalar2=None,
                                    op0=ALU.mult)
                    return e.tensor_scalar(out=st["v1"][:, :],
                                           in0=st["ssq"][:, :],
                                           scalar1=1.0 / DM, scalar2=None,
                                           op0=ALU.mult)
                tC1 = p.step("vector", [tQ, tS], comb1, sem="cV")
                tC2 = p.step("vector", [tC1],
                             lambda e: e.tensor_tensor(
                                 out=st["v2"][:, :], in0=st["negmu"][:, :],
                                 in1=st["negmu"][:, :], op=ALU.mult),
                             sem="cV")
                tC3 = p.step("vector", [tC2],
                             lambda e: e.tensor_tensor(
                                 out=st["v1"][:, :], in0=st["v1"][:, :],
                                 in1=st["v2"][:, :], op=ALU.subtract),
                             sem="cV")
                tC4 = p.step("scalar", [tC3],
                             lambda e: e.activation(out=st["v2"][:, :],
                                                    in_=st["v1"][:, :],
                                                    func=AF.Sqrt,
                                                    bias=EPS5[:, :]),
                             sem="cA")
                tI = p.step("vector", [tC4],
                            lambda e: e.reciprocal(out=st["inv"][:, :],
                                                   in_=st["v2"][:, :]),
                            sem="cV")
                tL = p.step("vector", [tI],
                            lambda e, xr=xr: ts2(e, xr, xr, st["negmu"][:, :],
                                                 st["inv"][:, :], ALU.add,
                                                 ALU.mult),
                            sem="cV")
                st_free = tL
                tT_last = None
                for bi in range(4):
                    def trq(e, bi=bi, xr=xr):
                        last = None
                        for jj in range(4):
                            kc = bi * 4 + jj
                            last = e.transpose(psA[:, jj * 512:jj * 512 + 128],
                                               xr[:, kc * 128:(kc + 1) * 128],
                                               EYE[:])
                        return last
                    wts = [tL] + ([ev_last] if ev_last else []) + \
                        ([l1t_mm] if (l1t_mm and g == 0 and bi == 0) else [])
                    tT = p.step("tensor", wts, trq, sem="cT")
                    tT_last = tT

                    def evq(e, bi=bi, g=g):
                        last = None
                        for jj in range(4):
                            kc = bi * 4 + jj
                            last = e.tensor_copy(
                                out=L1T[:, kc * 512 + g * 128:
                                        kc * 512 + (g + 1) * 128],
                                in_=psA[:, jj * 512:jj * 512 + 128])
                        return last
                    tE = p.step("vector", [tT], evq, sem="cV")
                    ev_last = tE
                psA_free = ev_last
                xr_free[half] = tT_last
            def mmsct(e):
                last = None
                for kc in range(16):
                    last = e.matmul(psB[0:16, 1536:2048],
                                    WsT[:, kc * 16:(kc + 1) * 16],
                                    L1T[:, kc * 512:(kc + 1) * 512],
                                    start=(kc == 0), stop=(kc == 15))
                return last
            tSCm = p.step("tensor", [ev_last] + ([sct_evac] if sct_evac else []),
                          mmsct, sem="cT")
            tSCe = p.step("scalar", [tSCm],
                          lambda e, s=s: e.activation(
                              out=SCTB[0:16, s * 512:(s + 1) * 512],
                              in_=psB[0:16, 1536:2048], func=AF.Silu,
                              bias=BSV[:, :]),
                          sem="cA")
            sct_evac = tSCe
            zb_prev = None
            sq_prev = None
            for h in range(H):
                bank = h % 2
                col = bank * 512

                def mmq(e, h=h, col=col):
                    last = None
                    for kc in range(16):
                        last = e.matmul(
                            psB[:, col:col + 512],
                            WQO[:, (kc * 16 + h) * 128:(kc * 16 + h + 1) * 128],
                            L1T[:, kc * 512:(kc + 1) * 512],
                            start=(kc == 0), stop=(kc == 15))
                    return last
                wts = [ev_last, wq_tok] + \
                    ([sil_free[bank]] if sil_free[bank] else [])
                tM = p.step("tensor", wts, mmq, sem="cT")
                l1t_mm = tM
                tSil = p.step("scalar", [tM],
                              lambda e, h=h, s=s, col=col: e.activation(
                                  out=QHB[:, h * 2048 + s * 512:
                                          h * 2048 + (s + 1) * 512],
                                  in_=psB[:, col:col + 512], func=AF.Silu),
                              sem="cA")
                sil_free[bank] = tSil
                wsq = [tSil] + ([zb_prev] if zb_prev else [])
                tSq = p.step("scalar", wsq,
                             lambda e, h=h, s=s: e.activation(
                                 out=Q2T[:, :],
                                 in_=QHB[:, h * 2048 + s * 512:
                                         h * 2048 + (s + 1) * 512],
                                 func=AF.Square),
                             sem="cA")
                wzb = [tSq] + ([ivq_read] if (h == 0 and ivq_read) else [])
                tZb = p.step("tensor", wzb,
                             lambda e, h=h: e.matmul(
                                 psB[0:16, 1024:1536], Zb[:, 15 - h:31 - h],
                                 Q2T[:, :], start=(h == 0), stop=(h == 15)),
                             sem="cT")
                zb_prev = tZb
            tN1 = p.step("scalar", [zb_prev],
                         lambda e: e.activation(out=SQ[:, :],
                                                in_=psB[0:16, 1024:1536],
                                                func=AF.Sqrt,
                                                bias=EPSQ[0:16, :]),
                         sem="cA")
            ivq_read = tN1
            t_1 = p.step("vector", [tN1],
                         lambda e: e.tensor_scalar(out=SQ[:, :], in0=SQ[:, :],
                                                   scalar1=1e-5, scalar2=None,
                                                   op0=ALU.add),
                         sem="cV")
            t_2 = p.step("vector", [t_1],
                         lambda e: e.reciprocal(out=SA[:, :], in_=SQ[:, :]),
                         sem="cV")
            tIVQ = p.step("vector", [t_2],
                          lambda e: e.tensor_copy(out=IVQ[:, :], in_=SA[:, :]),
                          sem="cV")
            bc_prev = [None, None]
            for h in range(H):
                bank = h % 2
                col = bank * 512
                wts = [tIVQ] + ([bc_prev[bank]] if bc_prev[bank] else [])
                tBC = p.step("tensor", wts,
                             lambda e, h=h, col=col: e.matmul(
                                 psB[:, col:col + 512],
                                 RS1[:, h * 128:(h + 1) * 128], IVQ[:, :],
                                 start=True, stop=True),
                             sem="cT")
                tTT = p.step("vector", [tBC],
                             lambda e, h=h, s=s, col=col: e.tensor_tensor(
                                 out=QHB[:, h * 2048 + s * 512:
                                         h * 2048 + (s + 1) * 512],
                                 in0=QHB[:, h * 2048 + s * 512:
                                         h * 2048 + (s + 1) * 512],
                                 in1=psB[:, col:col + 512], op=ALU.mult),
                             sem="cV")
                bc_prev[bank] = tTT
        SCTBr = SCTB[:, :].rearrange("p (r s) -> p s r", s=16)
        tPm = None
        for h in range(H):
            w = [sct_evac] if h == 0 else [tPm]
            tPm = p.step("vector", w,
                         lambda e, h=h: e.tensor_copy(
                             out=SCTH[0:16, h * 128:(h + 1) * 128],
                             in_=SCTBr[0:16, h, :]),
                         sem="cV")
        if bb == 0:
            dbg_store(p, io, "qhb", p.tok("cV"), QHB[:, 0:4096])
            dbg_store(p, io, "sctb", tPm, SCTB[0:16, :])
            dbg_store(p, io, "scth", tPm, SCTH[0:16, :])
        if STOPAT == "phasea":
            return None
        p.barrier()

        def ld_wo(e):
            ins = []
            for j in range(16):
                ins.append(e.dma_start(
                    out=WQO[:, j * 2048:(j + 1) * 2048],
                    in_=io["Wo"][j * 128:(j + 1) * 128, :]))
            return ins
        ld_wo.ndma = 16
        wo_tok = p.step("gpsimd", [], ld_wo, sem="dG", dma=True)

        g_free = None
        gt_free = None
        o_free = None
        of_free = [None] * 4
        o2_free = None
        s_free = None
        bci_free = None
        bcn_free = None
        bcs_free = None
        sst_free = None
        for j in range(16):
            g0_toks = []
            of_toks = []
            s1_tok = None
            s2_tok = None
            for mu in range(4):
                mv = QHB3[:, 4 * mu:4 * mu + 4, j * 128:(j + 1) * 128]
                wts = ([g_free] if g_free else [])
                tG0 = p.step("tensor", wts,
                             lambda e, j=j, mv=mv: e.matmul(
                                 psA[:, 0:512], W0T[:, j * 128:(j + 1) * 128],
                                 mv, start=True, stop=True),
                             sem="cT")
                tG2 = p.step("tensor", [],
                             lambda e, j=j, mv=mv: e.matmul(
                                 psA[:, 512:1024], W2T[:, j * 128:(j + 1) * 128],
                                 mv, start=True, stop=True),
                             sem="cT")
                g0_toks.append((tG0, tG2))
                tGT0 = p.step("scalar", [tG0],
                              lambda e: e.activation(out=GT0[:, :],
                                                     in_=psA[:, 0:512],
                                                     func=AF.Silu),
                              sem="cA")
                tGT = p.step("vector",
                             [tGT0, tG2] + ([gt_free] if gt_free else []),
                             lambda e: e.tensor_tensor(
                                 out=GT[:, :], in0=GT0[:, :],
                                 in1=psA[:, 512:1024], op=ALU.mult),
                             sem="cV")
                g_free = tGT
                wts = [tGT] + ([o_free] if o_free else [])
                tO = p.step("tensor", wts,
                            lambda e, j=j: e.matmul(
                                psA[:, 1024:1536],
                                W1T[:, j * 128:(j + 1) * 128], GT[:, :],
                                start=True, stop=True),
                            sem="cT")
                gt_free = tO
                tOF = p.step("scalar",
                             [tO] + ([of_free[mu]] if of_free[mu] else []),
                             lambda e, mu=mu: e.activation(
                                 out=OF[:, mu * 512:(mu + 1) * 512],
                                 in_=psA[:, 1024:1536], func=AF.Copy),
                             sem="cA")
                tO2 = p.step("scalar",
                             [tOF] + ([o2_free] if o2_free else []),
                             lambda e: e.activation(out=O2[:, :],
                                                    in_=psA[:, 1024:1536],
                                                    func=AF.Square),
                             sem="cA")
                o_free = tO2
                of_toks.append(tOF)
                wts = [tOF] + ([s_free] if (mu == 0 and s_free) else [])
                s1_tok = p.step("tensor", wts + ([s1_tok] if s1_tok else []),
                                lambda e, mu=mu: e.matmul(
                                    psB[0:4, 0:512],
                                    ONES4[:, mu * 4:(mu + 1) * 4],
                                    OF[:, mu * 512:(mu + 1) * 512],
                                    start=(mu == 0), stop=(mu == 3)),
                                sem="cT")
                s2_tok = p.step("tensor", [tO2] + ([s2_tok] if s2_tok else []),
                                lambda e, mu=mu: e.matmul(
                                    psB[0:4, 512:1024],
                                    ONES4[:, mu * 4:(mu + 1) * 4], O2[:, :],
                                    start=(mu == 0), stop=(mu == 3)),
                                sem="cT")
                o2_free = s2_tok
            wts = [s1_tok] + ([sst_free] if sst_free else [])
            tNMU = p.step("vector", wts,
                          lambda e: e.tensor_scalar(
                              out=SSTA[:, :], in0=psB[0:4, 0:512],
                              scalar1=-1.0 / 128, scalar2=None, op0=ALU.mult),
                          sem="cV")
            tMU2 = p.step("vector", [tNMU],
                          lambda e: e.tensor_tensor(
                              out=SSTB[:, :], in0=SSTA[:, :],
                              in1=SSTA[:, :], op=ALU.mult),
                          sem="cV")
            tV = p.step("vector", [tMU2, s2_tok],
                        lambda e: e.scalar_tensor_tensor(
                            out=SSTB[:, :], in0=psB[0:4, 512:1024],
                            scalar=1.0 / 128, in1=SSTB[:, :],
                            op0=ALU.mult, op1=ALU.subtract),
                        sem="cV")
            s_free = tV
            tSD = p.step("scalar", [tV],
                         lambda e: e.activation(out=SQ[0:4, :],
                                                in_=SSTB[:, :],
                                                func=AF.Sqrt,
                                                bias=EPS6[0:4, :]),
                         sem="cA")
            tIV = p.step("vector", [tSD],
                         lambda e: e.reciprocal(out=SA[0:4, :],
                                                in_=SQ[0:4, :]),
                         sem="cV")
            tIVb = p.step("vector", [tIV],
                          lambda e: e.tensor_copy(out=SST[:, 0:512],
                                                  in_=SA[0:4, :]),
                          sem="cV")
            tNM = p.step("vector", [tIVb],
                         lambda e: e.tensor_tensor(
                             out=SST[:, 512:1024], in0=SSTA[:, :],
                             in1=SA[0:4, :], op=ALU.mult),
                         sem="cV")
            tt4_toks = []
            for mu in range(4):
                wts = [tNM] + ([bci_free] if bci_free else [])
                tBCI = p.step("tensor", wts,
                              lambda e, mu=mu: e.matmul(
                                  psB[:, 1024:1536],
                                  RSB4[:, mu * 128:(mu + 1) * 128],
                                  SST[0:4, 0:512], start=True, stop=True),
                              sem="cT")
                wts = [tNM] + ([bcn_free] if bcn_free else [])
                tBCN = p.step("tensor", wts,
                              lambda e, mu=mu: e.matmul(
                                  psB[:, 1536:2048],
                                  RSB4[:, mu * 128:(mu + 1) * 128],
                                  SST[0:4, 512:1024], start=True, stop=True),
                              sem="cT")
                tT1 = p.step("vector", [tBCI],
                             lambda e, mu=mu: e.tensor_tensor(
                                 out=TB1[:, :],
                                 in0=OF[:, mu * 512:(mu + 1) * 512],
                                 in1=psB[:, 1024:1536], op=ALU.mult),
                             sem="cV")
                bci_free = tT1
                of_free[mu] = tT1
                tT2 = p.step("vector", [tT1, tBCN],
                             lambda e: e.tensor_tensor(
                                 out=TB1[:, :], in0=TB1[:, :],
                                 in1=psB[:, 1536:2048], op=ALU.add),
                             sem="cV")
                bcn_free = tT2
                tT3 = p.step("scalar", [tT2],
                             lambda e: e.activation(out=TB2[:, :],
                                                    in_=TB1[:, :],
                                                    func=AF.Identity,
                                                    scale=LNG[:, :],
                                                    bias=LNB[:, :]),
                             sem="cA")
                wts = ([tPm] if (j == 0 and mu == 0) else []) + \
                    ([bcs_free] if bcs_free else [])
                tBCS = p.step("tensor", wts,
                              lambda e, j=j, mu=mu: e.matmul(
                                  psA[:, 1536:2048],
                                  RS1[:, j * 128:(j + 1) * 128],
                                  SCTH[:, mu * 512:(mu + 1) * 512],
                                  start=True, stop=True),
                              sem="cT")
                TB23 = TB2[:, :].rearrange("p (i t) -> p i t", t=128)
                PS3 = psA[:, 1536:2048].rearrange("p (i t) -> p i t", t=128)
                tT4 = p.step("vector",
                             [tT3, tBCS, g0_toks[mu][0], g0_toks[mu][1]],
                             lambda e, j=j, mu=mu, TB23=TB23, PS3=PS3: e.tensor_tensor(
                                 out=QHB3[:, 4 * mu:4 * mu + 4,
                                          j * 128:(j + 1) * 128],
                                 in0=TB23, in1=PS3, op=ALU.mult),
                             sem="cV")
                bcs_free = tT4
                tt4_toks.append(tT4)
            sst_free = tt4_toks[-1]
            if bb == 0 and j == 0:
                dbg_store(p, io, "oj", of_toks[-1], OF[:, :])
                dbg_store(p, io, "ojf", tt4_toks[-1], QHB[:, 0:2048])
        if STOPAT == "phaseb":
            return None
        p.barrier()

        xr_free = [None, None]
        ev_free = None
        y_mask_free = None
        rq_evac = None
        x1_free = None
        for m in range(16):
            half = m % 2
            xrm = XRH[half]
            xv = io["x"][:, :].rearrange("(b r s) d -> b s r d", b=2, s=16)
            tXr = dma_step(p, "sync", "dS",
                           ([xr_free[half]] if xr_free[half] else []),
                           [(xrm, xv[bb, m])])
            def mmwo(e, m=m):
                last = None
                for j in range(16):
                    for nu in range(4):
                        last = e.matmul(
                            psA[:, nu * 512:(nu + 1) * 512],
                            QHB[:, m * 2048 + j * 128:m * 2048 + (j + 1) * 128],
                            WQO[:, j * 2048 + nu * 512:j * 2048 + (nu + 1) * 512],
                            start=(j == 0), stop=(j == 15))
                return last
            wts = [wo_tok] + ([ev_free] if ev_free else [])
            tWmm = p.step("tensor", wts, mmwo, sem="cT")
            ev_toks = []
            for nu in range(4):
                wev = [tWmm] + ([y_mask_free] if (nu == 0 and y_mask_free)
                                else [])
                tEv = p.step("scalar", wev,
                             lambda e, nu=nu: e.activation(
                                 out=Y[:, nu * 512:(nu + 1) * 512],
                                 in_=psA[:, nu * 512:nu * 512 + 512],
                                 func=AF.Copy),
                             sem="cA")
                ev_toks.append(tEv)
            ev_free = ev_toks[-1]
            if bb == 0 and m == 0:
                dbg_store(p, io, "y", ev_toks[-1], Y[:, :])
            wts = list(ev_toks) + ([x1_free] if x1_free else [])
            tIy = ln_stats(p, Y[:, :], DM, X1[:, :], EPS5[:, :],
                           st["negmu"][:, :], st["inv"][:, :], st,
                           wts, f"lny_{bb}_{m}")
            tNy = p.step("vector", [tIy],
                         lambda e: ts2(e, Y[:, :], Y[:, :], st["negmu"][:, :],
                                       st["inv"][:, :], ALU.add, ALU.mult),
                         sem="cV")
            tX1 = p.step("vector", [tNy, tXr],
                         lambda e, xrm=xrm: e.tensor_tensor(
                             out=X1[:, :], in0=xrm, in1=Y[:, :], op=ALU.add),
                         sem="cV")
            slot = bb * 16 + m
            tSt1 = dma_step(p, "scalar", "dA", [tX1],
                            [(io["x1s"][slot * 128:(slot + 1) * 128, :],
                              X1[:, :])])
            tI2 = ln_stats(p, X1[:, :], DM, Y[:, :], EPS5[:, :],
                           st["negmu"][:, :], st["inv"][:, :], st,
                           [tX1], f"ln2_{bb}_{m}")
            tN2 = p.step("vector", [tI2],
                         lambda e: ts2(e, Y[:, :], X1[:, :], st["negmu"][:, :],
                                       st["inv"][:, :], ALU.add, ALU.mult),
                         sem="cV")
            tQa = p.step("vector", [tN2],
                         lambda e: e.reduce_max(out=st["m1"][:, :], in_=Y[:, :],
                                                axis=AX.X,
                                                apply_absolute_value=True),
                         sem="cV")
            tQb = p.step("vector", [tQa],
                         lambda e, m=m: e.tensor_scalar(
                             out=M1C[:, m:m + 1], in0=st["m1"][:, :],
                             scalar1=1e-5, scalar2=None, op0=ALU.max),
                         sem="cV")
            tQc = p.step("vector", [tQb],
                         lambda e, m=m: e.reciprocal(
                             out=st["v1"][:, :], in_=M1C[:, m:m + 1]),
                         sem="cV")
            tQs = p.step("vector", [tQc],
                         lambda e: e.tensor_scalar(out=st["qs"][:, :],
                                                   in0=st["v1"][:, :],
                                                   scalar1=127.0, scalar2=None,
                                                   op0=ALU.mult),
                         sem="cV")
            tZa = p.step("vector", [tQs],
                         lambda e: ts2(e, Z0[:, :], Y[:, :], st["qs"][:, :],
                                       127.0, ALU.mult, ALU.add),
                         sem="cV")
            tZ = p.step("vector", [tZa],
                        lambda e, xrm=xrm: e.tensor_copy(out=xrm, in_=Z0[:, :]),
                        sem="cV")
            tLad = ladder(p, xrm, Y[:, :], DM, [tZ], f"lad_{bb}_{m}")
            y_mask_free = tLad
            tRa = p.step("vector", [tLad, tSt1],
                         lambda e, xrm=xrm: e.tensor_tensor(
                             out=X1[:, :], in0=Z0[:, :], in1=xrm,
                             op=ALU.subtract),
                         sem="cV")
            tR = p.step("vector", [tRa],
                        lambda e: e.tensor_scalar(out=X1[:, :], in0=X1[:, :],
                                                  scalar1=-127.0, scalar2=None,
                                                  op0=ALU.add),
                        sem="cV")
            xr_free[half] = tRa
            tT = None
            for bi in range(4):
                def trr(e, bi=bi):
                    last = None
                    for jj in range(4):
                        kc = bi * 4 + jj
                        last = e.transpose(psB[:, jj * 512:jj * 512 + 128],
                                           X1[:, kc * 128:(kc + 1) * 128],
                                           EYE[:])
                    return last
                wts = [tR] + ([rq_evac] if rq_evac else [])
                tT = p.step("tensor", wts, trr, sem="cT")

                def evr(e, bi=bi):
                    last = None
                    for jj in range(4):
                        kc = bi * 4 + jj
                        last = e.tensor_copy(
                            out=RQT[:, kc * 128:(kc + 1) * 128],
                            in_=psB[:, jj * 512:jj * 512 + 128])
                    return last
                rq_evac = p.step("vector", [tT], evr, sem="cV")
            x1_free = tT

            def strq(e, slot=slot):
                src = RQT[:, :].rearrange("q (kc s) -> q kc s", s=128)
                dst = io["rqTs"][:, :].rearrange("(kc q) n -> q kc n", q=128)[
                    :, :, slot * 128:(slot + 1) * 128]
                return [e.dma_start(out=dst, in_=src)]
            strq.ndma = 1
            tSrq = p.step("scalar", [rq_evac], strq, sem="dA", dma=True)
            rq_evac = tSrq

            if m == 15:
                def stcs(e, bb=bb):
                    return [e.dma_start(out=io["cs1"][:, bb * 16:(bb + 1) * 16],
                                        in_=M1C[:, :])]
                stcs.ndma = 1
                tScs = p.step("scalar", [tQb], stcs, sem="dA", dma=True)
            last_store = tSrq
    return last_store



def p2_build(p, io, mk, psA, THR, MAB, RS0F, start_waits):
    nc = p.nc
    WB = [mk("p2_wb0", [128, FF], F32), mk("p2_wb1", [128, FF], F32)]
    TRN = mk("p2_trn", [128, FF], F32)
    TTB = mk("p2_ttb", [128, 64 * 128], BF16)
    ACC = mk("p2_acc", [128, 2], F32)
    VAq = mk("p2_va", [128, 2], F32)
    ONE = mk("p2_one", [128, 1], F32)
    MP = mk("p2_mp", [16, 2], F32)
    SCL = mk("p2_scl", [1, 8], F32)
    S45 = mk("p2_s45", [128, 1], F32)
    B45 = [mk("p2_b45a", [128, 1], F32), mk("p2_b45b", [128, 1], F32)]

    t0 = p.step("vector", start_waits,
                lambda e: [e.memset(ONE[:, :], 1.0), e.memset(MP[0:16, :], 0.0),
                           e.memset(S45[:, :], -POW45),
                           e.memset(ACC[:, :], 0.0)][-1],
                sem="cV")
    tR = start_waits[0] if start_waits else None

    toks = {}
    for wi, (key, nch, ncol, nsum) in enumerate(
            (("f_w1", 64, DM, FF * DM), ("f_w2", 16, FF, DM * FF))):
        tprev = [t0] + ([tR] if tR else [])
        tload = [None, None]
        tacc = t0
        for c in range(nch):
            buf = WB[c % 2]
            w = [tprev[0]] if c < 2 else [toks[("a", wi, c - 2)]]
            tload[c % 2] = dma_step(p, "sync", "dS", w + [t0],
                                    [(buf[:, 0:ncol],
                                      io[key][c * 128:(c + 1) * 128, :])])
            tRd = p.step("vector", [tload[c % 2]],
                         lambda e, buf=buf, c=c, ncol=ncol: e.reduce_sum(
                             out=VAq[:, c % 2:c % 2 + 1], in_=buf[:, 0:ncol],
                             axis=AX.X, apply_absolute_value=True),
                         sem="cV")
            tacc = p.step("vector", [tRd, tacc],
                          lambda e, c=c, wi=wi: e.tensor_tensor(
                              out=ACC[:, wi:wi + 1], in0=ACC[:, wi:wi + 1],
                              in1=VAq[:, c % 2:c % 2 + 1], op=ALU.add),
                          sem="cV")
            toks[("a", wi, c)] = tacc
        tPS = p.step("tensor", [tacc],
                     lambda e, wi=wi: e.matmul(psA[0:1, 0:1], ONE[:, 0:1],
                                               ACC[:, wi:wi + 1],
                                               start=True, stop=True),
                     sem="cT")
        tM1 = p.step("vector", [tPS],
                     lambda e, wi=wi, nsum=nsum: e.tensor_scalar(
                         out=SCL[0:1, 4 * wi:4 * wi + 1], in0=psA[0:1, 0:1],
                         scalar1=1.0 / nsum, scalar2=None, op0=ALU.mult),
                     sem="cV")
        tM2 = p.step("vector", [tM1],
                     lambda e, wi=wi: e.tensor_scalar(
                         out=SCL[0:1, 4 * wi + 1:4 * wi + 2],
                         in0=SCL[0:1, 4 * wi:4 * wi + 1],
                         scalar1=1e-5, scalar2=None, op0=ALU.max),
                     sem="cV")
        tM3 = p.step("vector", [tM2],
                     lambda e, wi=wi: e.tensor_scalar(
                         out=SCL[0:1, 4 * wi + 2:4 * wi + 3],
                         in0=SCL[0:1, 4 * wi + 1:4 * wi + 2],
                         scalar1=0.5, scalar2=None, op0=ALU.mult),
                     sem="cV")
        tP1 = p.step("vector", [tM3],
                     lambda e, wi=wi: e.tensor_copy(
                         out=MP[0:1, 0:1],
                         in_=SCL[0:1, 4 * wi + 2:4 * wi + 3]),
                     sem="cV")
        tP1b = p.step("vector", [tP1],
                      lambda e, wi=wi: e.tensor_copy(
                          out=MP[0:1, 1:2],
                          in_=SCL[0:1, 4 * wi + 1:4 * wi + 2]),
                      sem="cV")
        tBC = p.step("tensor", [tP1b],
                     lambda e: e.matmul(psA[:, 0:2], RS0F[:, :], MP[:, :],
                                        start=True, stop=True),
                     sem="cT")
        tTH = p.step("vector", [tBC],
                     lambda e, wi=wi: [e.tensor_copy(out=THR[wi][:, :],
                                                     in_=psA[:, 0:1]),
                                       e.tensor_copy(out=MAB[wi][:, :],
                                                     in_=psA[:, 1:2])][-1],
                     sem="cV")
        tB45 = p.step("vector", [tTH],
                      lambda e, wi=wi: e.tensor_scalar(
                          out=B45[wi][:, :], in0=THR[wi][:, :], scalar1=-POW45,
                          scalar2=None, op0=ALU.mult),
                      sem="cV")

        ev_tok = None
        st_tok = None
        last_tT = None
        for c in range(nch):
            buf = WB[c % 2]
            w = [tB45] if c < 2 else [toks[("b", wi, c - 2)]]
            if st_tok is not None:
                w = w + [st_tok]
            tld = dma_step(p, "sync", "dS", w,
                           [(buf[:, 0:ncol], io[key][c * 128:(c + 1) * 128, :])])
            tSg = p.step("scalar", [tld] + ([last_tT] if last_tT else []),
                         lambda e, buf=buf, ncol=ncol, wi=wi: e.activation(
                             out=TRN[:, 0:ncol], in_=buf[:, 0:ncol],
                             func=AF.Sigmoid, scale=S45[:, :], bias=B45[wi][:, :]),
                         sem="cA")
            tTn = p.step("vector", [tSg],
                         lambda e, buf=buf, ncol=ncol, wi=wi: e.scalar_tensor_tensor(
                             out=TRN[:, 0:ncol], in0=buf[:, 0:ncol],
                             scalar=THR[wi][:, :], in1=TRN[:, 0:ncol],
                             op0=ALU.is_ge, op1=ALU.subtract),
                         sem="cV")
            toks[("b", wi, c)] = tTn
            nbl = ncol // 128
            for bi in range(nbl // 4):
                def trp(e, bi=bi):
                    last = None
                    for j in range(4):
                        blk = bi * 4 + j
                        last = e.transpose(psA[:, j * 512:j * 512 + 128],
                                           TRN[:, blk * 128:(blk + 1) * 128],
                                           b_eye(p)[:, :])
                    return last
                wts = [tTn] + ([ev_tok] if ev_tok else [])
                tT = p.step("tensor", wts, trp, sem="cT")
                last_tT = tT

                def evt(e, bi=bi):
                    last = None
                    for j in range(4):
                        blk = bi * 4 + j
                        last = e.tensor_copy(
                            out=TTB[:, blk * 128:(blk + 1) * 128],
                            in_=psA[:, j * 512:j * 512 + 128])
                    return last
                ev_tok = p.step("vector", [tT], evt, sem="cV")
            okey = "wq1t" if wi == 0 else "wq2t"
            ostride = FF if wi == 0 else DM

            def sto(e, c=c, nbl=nbl, okey=okey, ostride=ostride):
                src = TTB[:, :].rearrange("q (dc j) -> q dc j", j=128)[:, 0:nbl, :]
                dst = io[okey][:, :].rearrange("(dc q) n -> q dc n", q=128)[
                    :, 0:nbl, c * 128:(c + 1) * 128]
                return [e.dma_start(out=dst, in_=src)]
            sto.ndma = 1
            wst = [ev_tok] + ([st_tok] if st_tok else [])
            st_tok = p.step("scalar", wst, sto, sem="dA", dma=True)
            ev_tok = st_tok
        toks[("done", wi)] = st_tok
    return dict(SCL=SCL,
                done=[toks[("done", 0)], toks[("done", 1)],
                      p.tok("cV"), p.tok("cT"), p.tok("cA")])


_EYE_CACHE = {}


def b_eye(p):
    return _EYE_CACHE[id(p)]


def set_eye(p, eye_handle):
    _EYE_CACHE[id(p)] = eye_handle



def p3_build(p, io, mk, psA, MAB1, RS0B, SG, SG2, CS, CB, start_waits):
    nc = p.nc
    NSEC = 8
    FS = FF // NSEC
    W1S = [mk("p3_w1s0", [128, 16 * FS], BF16), mk("p3_w1s1", [128, 16 * FS], BF16)]
    RQS = [mk("p3_rqs0", [128, 16 * 128], BF16), mk("p3_rqs1", [128, 16 * 128], BF16)]
    B1M = mk("p3_b1m", [16, 512], BF16)
    B1BC = mk("p3_b1bc", [128, FF], BF16)
    HP = mk("p3_hp", [128, 512], F32)
    GT = mk("p3_gt", [128, FS], BF16)
    GSQ = mk("p3_gsq", [128, FS], BF16)
    V1 = mk("p3_v1", [128, 2], F32)

    t0 = p.step("vector", start_waits,
                lambda e: [e.memset(B1M[0:16, :], 0.0), e.memset(SG[:, :], 0.0),
                           e.memset(SG2[:, :], 0.0)][-1], sem="cV")
    tCS = dma_step(p, "sync", "dS", start_waits, [(CS[:, :], io["cs1"][:, :])])
    tCb0 = p.step("vector", start_waits + [tCS],
                  lambda e: e.tensor_scalar(out=V1[:, 0:1], in0=MAB1[:, :],
                                            scalar1=1.0 / 127, scalar2=None,
                                            op0=ALU.mult), sem="cV")
    tCb = p.step("vector", [tCb0],
                 lambda e: e.tensor_scalar(out=CB[:, :], in0=CS[:, :],
                                           scalar1=V1[:, 0:1], scalar2=None,
                                           op0=ALU.mult), sem="cV")
    tprev = [t0]
    for i in range(16):
        tld = p.step("gpsimd", tprev,
                     lambda e, i=i: [e.dma_start(
                         out=B1M[0:1, :],
                         in_=io["b1r"][:, i * 512:(i + 1) * 512])],
                     sem="dG", dma=True)
        tm = p.step("tensor", [tld] + tprev,
                    lambda e: e.matmul(psA[:, 0:512], RS0B, B1M[:, :],
                                       start=True, stop=True),
                    sem="cT")
        tev = p.step("vector", [tm],
                     lambda e, i=i: e.tensor_copy(
                         out=B1BC[:, i * 512:(i + 1) * 512], in_=psA[:, 0:512]),
                     sem="cV")
        tprev = [tev]
    tB1BC = tprev[0]
    if io.get("dbg3") is not None:
        dma_step(p, "gpsimd", "dG", [tB1BC], [(io["dbg3"][:, :], B1BC[:, :])])

    sg_tok = {}
    ld_w = [None, None]
    ld_r = {}
    st_g = None
    ev_tok = [tB1BC, tB1BC]
    hp_free = None
    for sec in range(NSEC):
        wbuf = W1S[sec % 2]
        w = [tB1BC, tCb] if sec < 2 else [sg_tok[(sec - 2, NSB - 1, "g")]]

        def ldw(e, sec=sec, wbuf=wbuf):
            src = io["wq1t"][:, :].rearrange("(kc q) n -> q kc n", q=128)[
                :, :, sec * FS:(sec + 1) * FS]
            return [e.dma_start(out=wbuf[:, :].rearrange("q (kc n) -> q kc n", n=FS),
                                in_=src)]
        ldw.ndma = 1
        ld_w[sec % 2] = p.step("sync", w, ldw, sem="dS", dma=True)
        tW = ld_w[sec % 2]
        for sb in range(NSB):
            rbuf = RQS[sb % 2]
            g = sec * NSB + sb
            w = []
            if g >= 2:
                ps, pb = divmod(g - 2, NSB)
                w = [sg_tok[(ps, pb, "mm")]]

            def ldr(e, sb=sb, rbuf=rbuf):
                src = io["rqTs"][:, :].rearrange("(kc q) n -> q kc n", q=128)[
                    :, :, sb * 128:(sb + 1) * 128]
                return [e.dma_start(
                    out=rbuf[:, :].rearrange("q (kc n) -> q kc n", n=128),
                    in_=src)]
            ldr.ndma = 1
            tR = p.step("scalar", w, ldr, sem="dA", dma=True)
            ld_r[(sec, sb)] = tR

            gelu_toks = []
            for fs in range(2):
                bank = fs
                def mmh(e, fs=fs, bank=bank, rbuf=rbuf, wbuf=wbuf):
                    last = None
                    for kc in range(16):
                        last = e.matmul(
                            psA[:, bank * 512:bank * 512 + 512],
                            rbuf[:, kc * 128:(kc + 1) * 128],
                            wbuf[:, kc * FS + fs * 512: kc * FS + (fs + 1) * 512],
                            start=(kc == 0), stop=(kc == 15))
                    return last
                wm = [tR, tW]
                if ev_tok[bank] is not None:
                    wm.append(ev_tok[bank])
                tM = p.step("tensor", wm, mmh, sem="cT")
                tEv = p.step("vector", [tM] + ([hp_free] if hp_free else []),
                             lambda e, fs=fs, bank=bank, sb=sb, sec=sec: e.scalar_tensor_tensor(
                                 out=HP[:, :], in0=psA[:, bank * 512:bank * 512 + 512],
                                 scalar=CB[:, sb:sb + 1],
                                 in1=B1BC[:, sec * FS + fs * 512:sec * FS + (fs + 1) * 512],
                                 op0=ALU.mult, op1=ALU.add),
                             sem="cV")
                ev_tok[bank] = tEv
                if sec == 0 and sb == 0 and fs == 0 and io.get("dbg4") is not None:
                    dma_step(p, "gpsimd", "dG", [tEv], [(io["dbg4"][:, :], HP[:, :])])
                tG = p.step("scalar", [tEv],
                            lambda e, fs=fs: e.activation(
                                out=GT[:, fs * 512:(fs + 1) * 512], in_=HP[:, :],
                                func=AF.Gelu),
                            sem="cA")
                ev_tok[bank] = tG
                hp_free = tG
                gelu_toks.append(tG)
            sg_tok[(sec, sb, "mm")] = gelu_toks[-1]
            def stg(e, sec=sec, sb=sb):
                return [e.dma_start(
                    out=io["gs"][sb * 128:(sb + 1) * 128, sec * FS:(sec + 1) * FS],
                    in_=GT[:, :])]
            stg.ndma = 1
            wst = gelu_toks + ([st_g] if st_g else [])
            st_g = p.step("scalar", wst, stg, sem="dA", dma=True)
            tSq = p.step("scalar", [st_g],
                         lambda e: e.activation(out=GSQ[:, :], in_=GT[:, :],
                                                func=AF.Square),
                         sem="cA")
            wr = [gelu_toks[-1]]
            if (sec - 1, sb, "s2") in sg_tok:
                wr.append(sg_tok[(sec - 1, sb, "s2")])
            tR1 = p.step("vector", wr,
                         lambda e: e.reduce_sum(out=V1[:, 0:1], in_=GT[:, :],
                                                axis=AX.X), sem="cV")
            tA1 = p.step("vector", [tR1],
                         lambda e, sb=sb: e.tensor_tensor(
                             out=SG[:, sb:sb + 1], in0=SG[:, sb:sb + 1],
                             in1=V1[:, 0:1], op=ALU.add), sem="cV")
            tR2 = p.step("vector", [tSq, tA1],
                         lambda e: e.reduce_sum(out=V1[:, 1:2], in_=GSQ[:, :],
                                                axis=AX.X), sem="cV")
            tA2 = p.step("vector", [tR2],
                         lambda e, sb=sb: e.tensor_tensor(
                             out=SG2[:, sb:sb + 1], in0=SG2[:, sb:sb + 1],
                             in1=V1[:, 1:2], op=ALU.add), sem="cV")
            sg_tok[(sec, sb, "s2")] = tA2
            sg_tok[(sec, sb, "g")] = tA2
    done = [st_g, sg_tok[(NSEC - 1, NSB - 1, "s2")], p.tok("cV"), p.tok("cA"),
            p.tok("cT")]
    return dict(done=done)



def p4_build(p, io, mk, psA, psB, EYE, MAB2, RS0B, SG, SG2, M2C, C2S,
             start_waits):
    nc = p.nc
    NG = NSB // 4
    GBUF = mk("p4_g0", [128, DM], BF16)
    U2F = mk("p4_u2f", [128, FF], F32)
    ZZF = mk("p4_zzf", [128, FF], F32)
    MASKF = mk("p4_maskf", [128, FF], F32)
    RQ2T = mk("p4_rq2t", [128, 4 * 64 * 128], BF16)
    W2S = [mk("p4_w2s0", [128, 512], BF16), mk("p4_w2s1", [128, 512], BF16)]
    GBM = mk("p4_gbm", [16, 512], BF16)
    GBC = mk("p4_gbc", [128, FF], BF16)
    BBC = mk("p4_bbc", [128, FF], BF16)
    B2BC = mk("p4_b2bc", [128, DM], BF16)
    CB2 = mk("p4_cb2", [128, 1], F32)
    EPS5b = mk("p4_eps5", [128, 1], F32)
    st4 = {k: mk("p4_st_" + k, [128, 1], F32)
           for k in ["nmu", "e2", "v2", "var", "inv", "m2", "qs", "s1", "s2"]}
    Y2 = [MASKF[:, i * DM:(i + 1) * DM] for i in range(4)]
    X1L = ZZF[:, 0:DM]

    t0 = p.step("vector", start_waits,
                lambda e: [e.memset(GBM[0:16, :], 0.0),
                           e.memset(EPS5b[:, :], 1e-5)][-1], sem="cV")
    tCB = p.step("vector", start_waits,
                 lambda e: e.tensor_scalar(out=CB2[:, :], in0=MAB2[:, :],
                                           scalar1=1.0 / 127, scalar2=None,
                                           op0=ALU.mult), sem="cV")
    tprev = t0
    for key, dst, n in (("gmr", GBC, FF), ("ber", BBC, FF), ("b2r", B2BC, DM)):
        for i in range(n // 512):
            tL = p.step("gpsimd", [tprev],
                        lambda e, key=key, i=i: [e.dma_start(
                            out=GBM[0:1, :],
                            in_=io[key][:, i * 512:(i + 1) * 512])],
                        sem="dG", dma=True)
            tm = p.step("tensor", [tL, tprev],
                        lambda e: e.matmul(psA[:, 0:512], RS0B, GBM[:, :],
                                           start=True, stop=True),
                        sem="cT")
            tprev = p.step("vector", [tm],
                           lambda e, i=i, dst=dst: e.tensor_copy(
                               out=dst[:, i * 512:(i + 1) * 512],
                               in_=psA[:, 0:512]),
                           sem="cV")
    tBC = tprev

    last_out = None
    last_tT = None
    last_ffn_mm = None
    last_evY = None
    y2_free = [None] * 4
    prev_tO = None
    for grp in range(NG):
        r_toks = []
        for sbq in range(4):
            sb = grp * 4 + sbq
            w1_ = [tBC, tCB]
            if prev_tO is not None:
                w1_.append(prev_tO)
            if last_tT is not None:
                w1_.append(last_tT)
            t1 = p.step("vector", w1_,
                        lambda e, sb=sb: [
                            e.tensor_scalar(out=st4["nmu"][:, :],
                                            in0=SG[:, sb:sb + 1],
                                            scalar1=-1.0 / FF, scalar2=None,
                                            op0=ALU.mult),
                            e.tensor_scalar(out=st4["e2"][:, :],
                                            in0=SG2[:, sb:sb + 1],
                                            scalar1=1.0 / FF, scalar2=None,
                                            op0=ALU.mult)][-1], sem="cV")
            t2_ = p.step("vector", [t1],
                         lambda e: e.tensor_tensor(out=st4["v2"][:, :],
                                                   in0=st4["nmu"][:, :],
                                                   in1=st4["nmu"][:, :],
                                                   op=ALU.mult), sem="cV")
            t3 = p.step("vector", [t2_],
                        lambda e: e.tensor_tensor(out=st4["var"][:, :],
                                                  in0=st4["e2"][:, :],
                                                  in1=st4["v2"][:, :],
                                                  op=ALU.subtract), sem="cV")
            t4 = p.step("scalar", [t3],
                        lambda e: e.activation(out=st4["v2"][:, :],
                                               in_=st4["var"][:, :], func=AF.Sqrt,
                                               bias=EPS5b[:, :]), sem="cA")
            t5 = p.step("vector", [t4],
                        lambda e: e.reciprocal(out=st4["inv"][:, :],
                                               in_=st4["v2"][:, :]), sem="cV")
            uq_tok = None
            g_ld = None
            for q in range(4):
                wg = [t5] + ([g_ld] if g_ld else [])
                g_ld = p.step("sync", wg,
                              lambda e, sb=sb, q=q: [e.dma_start(
                                  out=GBUF[:, :],
                                  in_=io["gs"][sb * 128:(sb + 1) * 128,
                                               q * DM:(q + 1) * DM])],
                              sem="dS", dma=True)
                tq = p.step("vector", [g_ld] + ([uq_tok] if uq_tok else []),
                            lambda e, q=q: e.scalar_tensor_tensor(
                                out=U2F[:, q * DM:(q + 1) * DM], in0=GBUF[:, :],
                                scalar=st4["nmu"][:, :],
                                in1=GBC[:, q * DM:(q + 1) * DM],
                                op0=ALU.add, op1=ALU.mult), sem="cV")
                g_ld = tq
                tu1 = p.step("vector", [tq],
                             lambda e, q=q: e.tensor_scalar(
                                 out=U2F[:, q * DM:(q + 1) * DM],
                                 in0=U2F[:, q * DM:(q + 1) * DM],
                                 scalar1=st4["inv"][:, :], scalar2=None,
                                 op0=ALU.mult), sem="cV")
                uq_tok = p.step("vector", [tu1],
                                lambda e, q=q: e.tensor_tensor(
                                    out=U2F[:, q * DM:(q + 1) * DM],
                                    in0=U2F[:, q * DM:(q + 1) * DM],
                                    in1=BBC[:, q * DM:(q + 1) * DM], op=ALU.add),
                                sem="cV")
            if sb == io.get("dbgsb", 0) and io.get("dbg5") is not None:
                dma_step(p, "gpsimd", "dG", [uq_tok], [(io["dbg5"][:, :], U2F[:, :])])
            tMx = p.step("vector", [uq_tok],
                         lambda e: e.reduce_max(out=st4["m2"][:, :], in_=U2F[:, :],
                                                axis=AX.X,
                                                apply_absolute_value=True),
                         sem="cV")
            tMc = p.step("vector", [tMx],
                         lambda e, sb=sb: e.tensor_scalar(
                             out=M2C[:, sb:sb + 1], in0=st4["m2"][:, :],
                             scalar1=1e-5, scalar2=None, op0=ALU.max), sem="cV")
            tC2 = p.step("vector", [tMc],
                         lambda e, sb=sb: e.tensor_scalar(
                             out=C2S[:, sb:sb + 1], in0=M2C[:, sb:sb + 1],
                             scalar1=CB2[:, :], scalar2=None, op0=ALU.mult),
                         sem="cV")
            tQ1 = p.step("vector", [tC2],
                         lambda e, sb=sb: e.reciprocal(
                             out=st4["v2"][:, :], in_=M2C[:, sb:sb + 1]),
                         sem="cV")
            tQs = p.step("vector", [tQ1],
                         lambda e: e.tensor_scalar(out=st4["qs"][:, :],
                                                   in0=st4["v2"][:, :],
                                                   scalar1=127.0, scalar2=None,
                                                   op0=ALU.mult), sem="cV")
            tZ0 = p.step("vector", [tQs],
                         lambda e: ts2(e, U2F[:, :], U2F[:, :], st4["qs"][:, :],
                                       127.0, ALU.mult, ALU.add), sem="cV")
            wzz = [tZ0]
            if prev_tO is not None:
                wzz.append(prev_tO)
            tZZ = p.step("vector", wzz,
                         lambda e: e.tensor_copy(out=ZZF[:, :], in_=U2F[:, :]),
                         sem="cV")
            wlad = [tZZ]
            if last_evY is not None:
                wlad.append(last_evY)
            if last_out is not None:
                wlad.append(last_out)
            tLad = ladder(p, ZZF[:, :], MASKF[:, :], FF, wlad, "lad2")
            tRa = p.step("vector", [tLad],
                         lambda e: e.tensor_tensor(out=MASKF[:, :], in0=U2F[:, :],
                                                   in1=ZZF[:, :], op=ALU.subtract),
                         sem="cV")
            tR = p.step("vector", [tRa],
                        lambda e: e.tensor_scalar(out=MASKF[:, :], in0=MASKF[:, :],
                                                  scalar1=-127.0, scalar2=None,
                                                  op0=ALU.add), sem="cV")
            if sb == io.get("dbgsb", 0) and io.get("dbg6") is not None:
                dma_step(p, "gpsimd", "dG", [tR], [(io["dbg6"][:, :], MASKF[:, :])])
            ev_tok = None
            for bi in range(16):
                def trr(e, bi=bi):
                    last = None
                    for j in range(4):
                        blk = bi * 4 + j
                        last = e.transpose(psB[:, j * 512:j * 512 + 128],
                                           MASKF[:, blk * 128:(blk + 1) * 128],
                                           EYE[:, :])
                    return last
                wts = [tR] + ([ev_tok] if ev_tok else [])
                if last_ffn_mm is not None and bi == 0 and sbq == 0:
                    wts.append(last_ffn_mm)
                tT = p.step("tensor", wts, trr, sem="cT")
                last_tT = tT

                def evr(e, bi=bi, sbq=sbq):
                    last = None
                    for j in range(4):
                        kc = bi * 4 + j
                        last = e.tensor_copy(
                            out=RQ2T[:, (sbq * 64 + kc) * 128:
                                     (sbq * 64 + kc + 1) * 128],
                            in_=psB[:, j * 512:j * 512 + 128])
                    return last
                ev_tok = p.step("vector", [tT], evr, sem="cV")
            r_toks.append(ev_tok)

        for eq in range(4):
            mm_toks = []
            for kc in range(64):
                wbuf = W2S[kc % 2]
                w = list(r_toks)
                if kc >= 2:
                    w = [mm_toks[kc - 2]]
                elif last_evY is not None:
                    w.append(last_evY)

                def ldp(e, kc=kc, eq=eq, wbuf=wbuf):
                    return [e.dma_start(
                        out=wbuf[:, :],
                        in_=io["wq2t"][kc * 128:(kc + 1) * 128,
                                       eq * 512:(eq + 1) * 512])]
                ldp.ndma = 1
                tWp = p.step("sync", w, ldp, sem="dS", dma=True)

                def mmf(e, kc=kc, eq=eq, wbuf=wbuf):
                    last = None
                    for tb in range(4):
                        last = e.matmul(
                            psA[:, tb * 512:(tb + 1) * 512],
                            RQ2T[:, (tb * 64 + kc) * 128:(tb * 64 + kc + 1) * 128],
                            wbuf[:, :], start=(kc == 0), stop=(kc == 63))
                    return last
                wm = [tWp]
                if kc == 0:
                    wm.append(p.tok("cV"))
                tMf = p.step("tensor", wm, mmf, sem="cT")
                mm_toks.append(tMf)
            last_ffn_mm = mm_toks[-1]
            for tb in range(4):
                sb = grp * 4 + tb
                tEy = p.step("vector", [mm_toks[-1]],
                             lambda e, tb=tb, sb=sb, eq=eq: e.scalar_tensor_tensor(
                                 out=Y2[tb][:, eq * 512:(eq + 1) * 512],
                                 in0=psA[:, tb * 512:(tb + 1) * 512],
                                 scalar=C2S[:, sb:sb + 1],
                                 in1=B2BC[:, eq * 512:(eq + 1) * 512],
                                 op0=ALU.mult, op1=ALU.add), sem="cV")
                last_evY = tEy
                if (grp == io.get("dbgsb", 0) // 4 and tb == io.get("dbgsb", 0) % 4
                        and eq == 3 and io.get("dbg7") is not None):
                    dma_step(p, "gpsimd", "dG", [tEy],
                             [(io["dbg7"][:, :], Y2[tb][:, :])])
        for tb in range(4):
            sb = grp * 4 + tb
            tIy = ln_stats(p, Y2[tb][:, :], DM, X1L[:, :], EPS5b[:, :],
                           st4["nmu"][:, :], st4["inv"][:, :], st4x(st4),
                           [last_evY] + ([prev_tO] if prev_tO else []), "lny2")
            tX = p.step("sync", [tIy],
                        lambda e, sb=sb: [e.dma_start(
                            out=X1L[:, :],
                            in_=io["x1s"][sb * 128:(sb + 1) * 128, :])],
                        sem="dS", dma=True)
            tN = p.step("vector", [tIy],
                        lambda e, tb=tb: ts2(e, Y2[tb][:, :], Y2[tb][:, :],
                                             st4["nmu"][:, :], st4["inv"][:, :],
                                             ALU.add, ALU.mult), sem="cV")
            tO = p.step("vector", [tN, tX],
                        lambda e, tb=tb: e.tensor_tensor(
                            out=Y2[tb][:, :], in0=X1L[:, :], in1=Y2[tb][:, :],
                            op=ALU.add), sem="cV")
            outv = io["out"][:, :].rearrange("(b r s) d -> b s r d", b=2, s=16)
            tSt = p.step("scalar", [tO] + ([last_out] if last_out else []),
                         lambda e, sb=sb, tb=tb, outv=outv: [e.dma_start(
                             out=outv[sb // 16, sb % 16],
                             in_=Y2[tb][:, :])], sem="dA", dma=True)
            last_out = tSt
            y2_free[tb] = tSt
            prev_tO = tO
    return dict(done=[last_out, p.tok("cV"), p.tok("cT"), p.tok("cA")])


def st4x(st4):
    return {"ssum": st4["s1"], "ssq": st4["s2"], "v1": st4["var"],
            "v2": st4["v2"]}



def build_full(nc, io):
    from contextlib import ExitStack
    ctx = ExitStack()
    p = Prog(nc, ctx)
    sb = lambda n, s, d=F32: ctx.enter_context(nc.sbuf_tensor(n, s, d))

    RS0F = sb("ps_rs0f", [16, 128], F32)
    PEYE = sb("ps_eye", [128, 128], F32)
    PRS0B = sb("ps_rs0b", [16, 128], BF16)
    SG = sb("ps_sg", [128, NSB], F32)
    SG2 = sb("ps_sg2", [128, NSB], F32)
    CSx = sb("ps_cs", [128, NSB], F32)
    CBx = sb("ps_cb", [128, NSB], F32)
    M2C = sb("ps_m2c", [128, NSB], F32)
    C2S = sb("ps_c2s", [128, NSB], F32)
    THR = [sb("ps_thr1", [128, 1], F32), sb("ps_thr2", [128, 1], F32)]
    MAB = [sb("ps_ma1", [128, 1], F32), sb("ps_ma2", [128, 1], F32)]

    tPs = dma_step(p, "sync", "dS", [],
                   [(RS0F[:, :], io["rs0f"][:, :]),
                    (PEYE[:, :], io["eye"][:, :]),
                    (PRS0B[:, :], io["rs1"][0:16, 0:128])])

    b1 = p1_build(p, io)
    set_eye(p, b1["EYE"])
    p1_tiles(p, b1, io)
    p.barrier()

    ov2 = Overlay(nc, b1["WQO"])
    set_eye(p, PEYE)
    b2 = p2_build(p, io, ov2.mk, b1["psA"], THR, MAB, RS0F, [tPs, p.tok("cV")])
    p.barrier()

    ov3 = Overlay(nc, b1["WQO"])
    RS0B = PRS0B[:, :]
    b3 = p3_build(p, io, ov3.mk, b1["psA"], MAB[0], RS0B, SG, SG2, CSx, CBx,
                  [p.tok("cV")])
    p.barrier()

    ov4 = Overlay(nc, b1["WQO"])
    b4 = p4_build(p, io, ov4.mk, b1["psA"], b1["psB"], PEYE, MAB[1], RS0B,
                  SG, SG2, M2C, C2S, [p.tok("cV")])
    p.wait_only("gpsimd", [p.tok(s) for s in list(p.sems)])
    p.finalize()
    ctx.close()
    return nc



def _build_program():
    nc = bass.Bass("TRN2", target_bir_lowering=False,
                   detect_race_conditions=False)
    dr = lambda n, s, d=F32, k="ExternalInput": nc.dram_tensor(
        n, s, d, kind=k).ap()
    io = dict(
        x=dr("x", [NTOK, DM]),
        Wqkv=dr("Wqkv", [DM, 3 * DM]),
        Wo=dr("Wo", [DM, DM]),
        Ws=dr("Ws", [DM, 16]),
        bs_c=dr("bs_c", [16, 1]),
        lng_c=dr("lng_c", [128, 1]),
        lnb_c=dr("lnb_c", [128, 1]),
        w0=dr("w0", [16, 128, 128]),
        w1=dr("w1", [16, 128, 128]),
        w2=dr("w2", [16, 128, 128]),
        eye=dr("eye", [128, 128]),
        rs1=dr("rs1", [16, 16 * 128], BF16),
        rsb4=dr("rsb4", [4, 512], BF16),
        rs0f=dr("rs0f", [16, 128]),
        f_w1=dr("f_w1", [FF, DM]),
        f_w2=dr("f_w2", [DM, FF]),
        b1r=dr("b1r", [1, FF]),
        gmr=dr("gmr", [1, FF]),
        ber=dr("ber", [1, FF]),
        b2r=dr("b2r", [1, DM]),
        out=dr("out", [NTOK, DM], F32, "ExternalOutput"),
        x1s=dr("x1s", [NTOK, DM], F32, "Internal"),
        rqTs=dr("rqTs", [DM, NTOK], BF16, "Internal"),
        cs1=dr("cs1", [128, NSB], F32, "Internal"),
        gs=dr("gs", [NTOK, FF], BF16, "Internal"),
        wq1t=dr("wq1t", [DM, FF], BF16, "Internal"),
        wq2t=dr("wq2t", [FF, DM], BF16, "Internal"),
    )
    build_full(nc, io)
    return nc


def kernel(**inputs):
    import ml_dtypes
    from concourse.bass_utils import run_bass_kernel_spmd
    f32 = np.float32
    bf = ml_dtypes.bfloat16
    NC = 8
    x = np.asarray(inputs["L0_attn_x"], dtype=f32).reshape(-1, DM)
    common = dict(
        Wqkv=np.asarray(inputs["L0_attn_W_qkv"], f32),
        Wo=np.asarray(inputs["L0_attn_W_o"], f32),
        Ws=np.asarray(inputs["L0_attn_W_s"], f32),
        bs_c=np.asarray(inputs["L0_attn_b_s"], f32).reshape(16, 1),
        lng_c=np.asarray(inputs["L0_attn_ln_g"], f32).reshape(128, 1),
        lnb_c=np.asarray(inputs["L0_attn_ln_b"], f32).reshape(128, 1),
        w0=np.asarray(inputs["L0_attn_w0"], f32),
        w1=np.asarray(inputs["L0_attn_w1"], f32),
        w2=np.asarray(inputs["L0_attn_w2"], f32),
        eye=np.eye(128, dtype=f32),
        rs1=np.kron(np.eye(16, dtype=f32), np.ones((1, 128), f32)).astype(bf),
        rsb4=np.kron(np.eye(4, dtype=f32), np.ones((1, 128), f32)).astype(bf),
        rs0f=np.concatenate([np.ones((1, 128), f32),
                             np.zeros((15, 128), f32)]),
        f_w1=np.asarray(inputs["L0_ffn_w1"], f32),
        f_w2=np.asarray(inputs["L0_ffn_w2"], f32),
        b1r=np.asarray(inputs["L0_ffn_b1"], f32).reshape(1, FF),
        gmr=np.asarray(inputs["L0_ffn_gamma"], f32).reshape(1, FF),
        ber=np.asarray(inputs["L0_ffn_beta"], f32).reshape(1, FF),
        b2r=np.asarray(inputs["L0_ffn_b2"], f32).reshape(1, DM),
    )
    nc = _build_program()
    ims = []
    for c in range(NC):
        im = dict(common)
        im["x"] = np.ascontiguousarray(x[c * NTOK:(c + 1) * NTOK])
        ims.append(im)
    res = run_bass_kernel_spmd(nc, ims, core_ids=list(range(NC)))
    out = np.concatenate(
        [np.asarray(res.results[c]["out"], dtype=np.float32)
         for c in range(NC)], axis=0)
    return out.reshape(16, 2048, DM)


# version 06 (final)
# speedup vs baseline: 1.0371x; 1.0371x over previous
import numpy as np
import concourse.bass as bass
import concourse.mybir as mybir

F32 = mybir.dt.float32
BF16 = mybir.dt.bfloat16
AF = mybir.ActivationFunctionType
ALU = mybir.AluOpType
AX = mybir.AxisListType

NTOK = 4096
DM = 2048
H = 16
D = 128
FF = 8192
TT = 256
NT1 = NTOK // TT
NSB = NTOK // 128
POW45 = float(2.0 ** 45)

ENGS = ("sync", "gpsimd", "vector", "scalar", "tensor")


class Prog:
    def __init__(self, nc, ctx):
        self.nc = nc
        self.ctx = ctx
        self.q = {e: [] for e in ENGS}
        self.sems = {}
        self.cnt = {}
        self.lastw = {}
        self._bar = []

    def sem(self, name):
        if name not in self.sems:
            self.sems[name] = self.ctx.enter_context(self.nc.semaphore(name))
            self.cnt[name] = 0
        return self.sems[name]

    def tok(self, name):
        return (name, self.cnt[name])

    def step(self, eng, waits, emit, sem=None, dma=False):
        waits = [(s, v) for (s, v) in list(waits) + self._bar if v > 0]
        eff = []
        for s, v in waits:
            k = (eng, s)
            if self.lastw.get(k, 0) < v:
                eff.append((s, v))
                self.lastw[k] = v
        if sem is not None:
            sh = self.sem(sem)

        def run(e, eff=tuple(eff), emit=emit, sem=sem, dma=dma):
            for s, v in eff:
                e.wait_ge(self.sems[s], v)
            last = emit(e)
            if sem is not None:
                if dma:
                    insts = last if isinstance(last, (list, tuple)) else [last]
                    for ins in insts:
                        ins.then_inc(self.sems[sem], 16)
                else:
                    if isinstance(last, (list, tuple)):
                        last = last[-1]
                    if last is None:
                        last = e.nop()
                    last.then_inc(self.sems[sem], 1)
        self.q[eng].append(run)
        if sem is not None:
            if dma:
                n = getattr(emit, "ndma", 1)
                self.cnt[sem] += 16 * n
            else:
                self.cnt[sem] += 1
            return (sem, self.cnt[sem])
        return None

    def barrier(self):
        self._bar = [(s, self.cnt[s]) for s in self.sems]

    def wait_only(self, eng, waits):
        self.step(eng, waits, lambda e: None, sem=None)

    def finalize(self):
        nc = self.nc
        with nc.Block() as block:
            @block.sync
            def _(e):
                for f in self.q["sync"]:
                    f(e)

            @block.gpsimd
            def _(e):
                for f in self.q["gpsimd"]:
                    f(e)

            @block.vector
            def _(e):
                for f in self.q["vector"]:
                    f(e)

            @block.scalar
            def _(e):
                for f in self.q["scalar"]:
                    f(e)

            @block.tensor
            def _(e):
                for f in self.q["tensor"]:
                    f(e)


def dma_step(p, eng, sem, waits, pairs):
    def emit(e):
        return [e.dma_start(out=o, in_=i) for (o, i) in pairs]
    emit.ndma = len(pairs)
    return p.step(eng, waits, emit, sem=sem, dma=True)



def ts2(e, out, in0, s1, s2, op0, op1):
    return e.tensor_scalar(out=out, in0=in0, scalar1=s1, scalar2=s2,
                           op0=op0, op1=op1)



def ln_stats(p, x_ap, n, dump, eps_ap, negmu, inv, sc, waits, tag):
    import os
    if os.environ.get("VSQ"):
        tA = p.step("vector", waits,
                    lambda e: e.tensor_tensor(out=dump, in0=x_ap, in1=x_ap,
                                              op=ALU.mult),
                    sem="cV")
    else:
        tA = p.step("scalar", waits,
                    lambda e: e.activation(out=dump, in_=x_ap, func=AF.Square),
                    sem="cA")
    tV = p.step("vector", waits,
                lambda e: e.reduce_sum(out=sc["ssum"][:, :], in_=x_ap, axis=AX.X),
                sem="cV")
    tV = p.step("vector", [tA, tV],
                lambda e: e.reduce_sum(out=sc["ssq"][:, :], in_=dump, axis=AX.X),
                sem="cV")

    def comb1(e):
        e.tensor_scalar(out=negmu, in0=sc["ssum"][:, :], scalar1=-1.0 / n,
                        scalar2=None, op0=ALU.mult)
        return e.tensor_scalar(out=sc["v1"][:, :], in0=sc["ssq"][:, :],
                               scalar1=1.0 / n, scalar2=None, op0=ALU.mult)
    tC1 = p.step("vector", [tA, tV], comb1, sem="cV")
    tC2 = p.step("vector", [tC1],
                 lambda e: e.tensor_tensor(out=sc["v2"][:, :], in0=negmu,
                                           in1=negmu, op=ALU.mult),
                 sem="cV")
    tC = p.step("vector", [tC2],
                lambda e: e.tensor_tensor(out=sc["v1"][:, :], in0=sc["v1"][:, :],
                                          in1=sc["v2"][:, :], op=ALU.subtract),
                sem="cV")
    tS = p.step("scalar", [tC],
                lambda e: e.activation(out=sc["v2"][:, :], in_=sc["v1"][:, :],
                                       func=AF.Sqrt, bias=eps_ap),
                sem="cA")
    tI = p.step("vector", [tS],
                lambda e: e.reciprocal(out=inv, in_=sc["v2"][:, :]),
                sem="cV")
    return tI



def ladder(p, z, m, n, waits, tag):
    t = None
    for k in range(7, -1, -1):
        w = waits if t is None else [t]
        thr = float(2.0 ** k) - 0.5
        sc = -float(2.0 ** k)
        tM = p.step("vector", w,
                    lambda e, thr=thr, sc=sc: ts2(e, m, z, thr, sc,
                                                  ALU.is_ge, ALU.mult),
                    sem="cV")
        t = p.step("vector", [tM],
                   lambda e: e.tensor_tensor(out=z, in0=z, in1=m, op=ALU.add),
                   sem="cV")
    return t



class Overlay:
    def __init__(self, nc, base_handle):
        self.nc = nc
        self.base = nc.lookup_mloc(base_handle).addr
        self.cur = 0

    def mk(self, name, shape, dtype, at=None):
        import functools
        import operator
        nc = self.nc
        h = nc.virtual_sbuf_tensor(name, shape, dtype)
        nc._tensor(h.name, h.shape, h.dtype, type="SB")
        mloc = nc.lookup_mloc(h)
        mloc.allocated = not nc.target_bir_lowering
        per_part = functools.reduce(operator.mul, shape[1:]) * mybir.dt.size(dtype)
        if not hasattr(self, "addrs"):
            self.addrs = {}
        if at is not None:
            mloc.addr = self.addrs[at]
        else:
            mloc.addr = self.base + self.cur
            self.cur += (per_part + 31) // 32 * 32
            assert self.base + self.cur <= 229376 - 2048, (
                f"overlay overflow at {name}: {self.base + self.cur}")
        self.addrs[name] = mloc.addr
        return h




def ladder3(p, z, m, n, waits, tag):
    t = None
    for k in range(7, -1, -1):
        w = waits if t is None else [t]
        thr = float(2.0 ** k) - 0.5
        sc = -float(2.0 ** k)

        def lvl(e, thr=thr, sc=sc):
            ts2(e, m, z, thr, sc, ALU.is_ge, ALU.mult)
            return e.tensor_tensor(out=z, in0=z, in1=m, op=ALU.add)
        t = p.step("vector", w, lvl, sem="cV")
    return t


def ladder2(p, z, m, s45, b45, waits, tag):
    t = None
    for k in range(7, -1, -1):
        w = waits if t is None else [t]
        tM = p.step("scalar", w,
                    lambda e, k=k: e.activation(out=m, in_=z, func=AF.Sigmoid,
                                                scale=s45[:, :],
                                                bias=b45[:, 7 - k:8 - k]),
                    sem="cA")
        t = p.step("vector", [tM],
                   lambda e, k=k: e.scalar_tensor_tensor(
                       out=z, in0=m, scalar=-float(2.0 ** k), in1=z,
                       op0=ALU.mult, op1=ALU.add),
                   sem="cV")
    return t


def p1_build(p, io):
    nc = p.nc
    ctx = p.ctx
    sb = lambda n, s, d=F32: ctx.enter_context(nc.sbuf_tensor(n, s, d))

    WQO = sb("p1_wqo", [128, H * H * 128], BF16)
    QHB = sb("p1_qhb", [128, H * DM], BF16)
    L1T = sb("p1_l1t", [128, 16 * 512], BF16)
    XR = sb("p1_xr", [128, 2 * DM], F32)
    Z0 = sb("p1_z0", [128, DM], F32)
    RQT = sb("p1_rqt", [128, 16 * 128], BF16)
    ov = Overlay(nc, L1T)
    OF = ov.mk("p1_of", [128, DM], BF16)
    GT0 = ov.mk("p1_gt0", [128, 512], F32)
    GT = ov.mk("p1_gtt", [128, 512], BF16)
    O2 = ov.mk("p1_o2", [128, 512], BF16)
    TB2 = ov.mk("p1_tb2", [128, 512], BF16)
    TB1 = ov.mk("p1_tb1", [128, 512], F32)
    SST = ov.mk("p1_sst", [4, 1024], BF16)
    assert ov.cur <= 16 * 512 * 2, f"L1T overlay overflow {ov.cur}"
    Y = ov.mk("p1_Y", [128, DM], F32, at="p1_of")
    X1 = ov.mk("p1_X1", [128, DM], F32, at="p1_tb2")
    ovz = Overlay(nc, Z0)
    SCTB = ovz.mk("p1_sctb", [16, DM], BF16)
    SCTH = ovz.mk("p1_scth", [16, DM], BF16)
    ovr = Overlay(nc, RQT)
    SSTA = ovr.mk("p1_ssta", [4, 512], F32)
    SSTB = ovr.mk("p1_sstb", [4, 512], F32)
    W0T = sb("p1_w0t", [128, H * 128], BF16)
    W1T = sb("p1_w1t", [128, H * 128], BF16)
    W2T = sb("p1_w2t", [128, H * 128], BF16)
    WsT = sb("p1_ws", [128, 16 * 16], BF16)
    RS1 = sb("p1_rs1", [16, H * 128], BF16)
    RSB4 = sb("p1_rsb4", [4, 4 * 128], BF16)
    Zb = sb("p1_zb", [128, 31], BF16)
    ONES4 = sb("p1_ones4", [128, 16], BF16)
    LNG = sb("p1_lng", [128, 1], F32)
    LNB = sb("p1_lnb", [128, 1], F32)
    EYE = sb("p1_eye", [128, 128], F32)
    BSV = sb("p1_bs", [16, 1], F32)
    EPS5 = sb("p1_eps5", [128, 1], F32)
    EPS6 = sb("p1_eps6", [128, 1], F32)
    EPSQ = sb("p1_epsq", [16, 1], F32)
    S45L = sb("p1_s45", [128, 1], F32)
    B45L = sb("p1_b45", [128, 8], F32)
    M1C = sb("p1_m1c", [128, 16], F32)
    SQ = sb("p1_sq", [16, 512], F32)
    SA = sb("p1_sa", [16, 512], F32)
    IVQ = sb("p1_ivq", [16, 512], BF16)
    Q2T = sb("p1_q2t", [128, 512], BF16)
    st = {k: sb("p1_st_" + k, [128, 1], F32)
          for k in ["ssum", "ssq", "v1", "v2", "negmu", "inv", "m1", "qs"]}
    psA = ctx.enter_context(nc.psum_tensor("p1_psA", [128, DM], F32))
    psB = ctx.enter_context(nc.psum_tensor("p1_psB", [128, DM], F32))

    def init_consts(e):
        e.memset(Zb[:, 0:15], 0.0)
        e.memset(Zb[:, 15:16], 1.0)
        e.memset(Zb[:, 16:31], 0.0)
        e.memset(ONES4[:, :], 0.0)
        for mu in range(4):
            e.memset(ONES4[:, mu * 5:mu * 5 + 1], 1.0)
        e.memset(EPS5[:], 1e-5)
        e.memset(EPS6[:], 1e-6)
        e.memset(EPSQ[:], 0.0)
        e.memset(S45L[:], POW45)
        for k in range(8):
            e.memset(B45L[:, 7 - k:8 - k], -(2.0 ** k - 0.5) * POW45)
        return e.memset(st["ssum"][:, :], 0.0)
    tCon = p.step("vector", [], init_consts, sem="cV")

    def ld_wq(e):
        ins = []
        for h in range(H):
            for kc in range(16):
                ins.append(e.dma_start(
                    out=WQO[:, (kc * 16 + h) * 128:(kc * 16 + h + 1) * 128],
                    in_=io["Wqkv"][kc * 128:(kc + 1) * 128,
                                   h * 384:h * 384 + 128]))
        for kc in range(16):
            ins.append(e.dma_start(
                out=WsT[:, kc * 16:(kc + 1) * 16],
                in_=io["Ws"][kc * 128:(kc + 1) * 128, :]))
        return ins
    ld_wq.ndma = H * 16 + 16
    import os
    ld_wq_fn = ld_wq
    if os.environ.get("SKIPW"):
        ld_wq_fn = lambda e: [e.dma_start(out=WQO[:, 0:128],
                                          in_=io["Wqkv"][0:128, 0:128])]
        ld_wq_fn.ndma = 1
    tW = p.step("gpsimd", [tCon], ld_wq_fn, sem="dG", dma=True)

    def ld_small(e):
        ins = [e.dma_start(out=RS1[:, :], in_=io["rs1"][:, :]),
               e.dma_start(out=RSB4[:, :], in_=io["rsb4"][:, :]),
               e.dma_start(out=LNG[:, 0:1], in_=io["lng_c"][:, :]),
               e.dma_start(out=LNB[:, 0:1], in_=io["lnb_c"][:, :]),
               e.dma_start(out=EYE[:], in_=io["eye"][:, :]),
               e.dma_start(out=BSV[:, 0:1], in_=io["bs_c"][:, :])]
        return ins
    ld_small.ndma = 6
    tWs = p.step("sync", [], ld_small, sem="dS", dma=True)

    evac_tok = None
    wlist = [] if os.environ.get("SKIPW") else [(W0T, "w0"), (W1T, "w1"),
                                                (W2T, "w2")]
    XT0 = XR[:, 0:DM]
    for i, (WT, key) in enumerate(wlist):
        def ldw(e, key=key):
            src_ap = io[key][:, :, :].rearrange("h q c -> q h c")
            return [e.dma_start(out=XT0, in_=src_ap)]
        ldw.ndma = 1
        wts = [tWs, tCon] + ([evac_tok] if evac_tok else [])
        tLd = p.step("sync", wts, ldw, sem="dS", dma=True)
        for bi in range(4):
            def trw(e, bi=bi):
                last = None
                for jj in range(4):
                    hh = bi * 4 + jj
                    last = e.transpose(psA[:, jj * 512:jj * 512 + 128],
                                       XT0[:, hh * 128:(hh + 1) * 128], EYE[:])
                return last
            wts = [tLd] + ([evac_tok] if evac_tok else [])
            tT = p.step("tensor", wts, trw, sem="cT")

            def evw(e, bi=bi, WT=WT):
                last = None
                for jj in range(4):
                    hh = bi * 4 + jj
                    last = e.tensor_copy(
                        out=WT[:, hh * 128:(hh + 1) * 128],
                        in_=psA[:, jj * 512:jj * 512 + 128])
                return last
            evac_tok = p.step("vector", [tT], evw, sem="cV")
    tWt = evac_tok if evac_tok is not None else tCon
    return dict(WQO=WQO, QHB=QHB, L1T=L1T, XR=XR, Z0=Z0, RQT=RQT, SCTB=SCTB,
                SCTH=SCTH, OF=OF, GT0=GT0, GT=GT, O2=O2, TB1=TB1, TB2=TB2,
                SST=SST, SSTA=SSTA, SSTB=SSTB, Y=Y, X1=X1, W0T=W0T, W1T=W1T, W2T=W2T,
                WsT=WsT, RS1=RS1, RSB4=RSB4, Zb=Zb, ONES4=ONES4, LNG=LNG,
                LNB=LNB, EYE=EYE, BSV=BSV, EPS5=EPS5, EPS6=EPS6, EPSQ=EPSQ,
                S45L=S45L, B45L=B45L, M1C=M1C, SQ=SQ, SA=SA, IVQ=IVQ, Q2T=Q2T,
                st=st, psA=psA, psB=psB, tInit=[tWt, tWs, tCon, tW], tWq=tW)


def dbg_store(p, io, name, tok, ap):
    if io.get("dbg") and name in io["dbg"]:
        return dma_step(p, "gpsimd", "dG", [tok], [(io["dbg"][name][:, :], ap)])
    return None


def p1_tiles(p, b, io):
    import os
    STOPAT = os.environ.get("STOPAT", "")
    WQO, QHB, L1T, XR, Z0 = b["WQO"], b["QHB"], b["L1T"], b["XR"], b["Z0"]
    RQT, SCTB, SCTH = b["RQT"], b["SCTB"], b["SCTH"]
    OF, GT0, GT, O2, TB1, TB2 = b["OF"], b["GT0"], b["GT"], b["O2"], b["TB1"], b["TB2"]
    SST, SSTA, SSTB, Y, X1 = b["SST"], b["SSTA"], b["SSTB"], b["Y"], b["X1"]
    W0T, W1T, W2T, WsT = b["W0T"], b["W1T"], b["W2T"], b["WsT"]
    RS1, RSB4, Zb, ONES4 = b["RS1"], b["RSB4"], b["Zb"], b["ONES4"]
    LNG, LNB, EYE, BSV = b["LNG"], b["LNB"], b["EYE"], b["BSV"]
    EPS5, EPS6, EPSQ = b["EPS5"], b["EPS6"], b["EPSQ"]
    S45L, B45L, M1C = b["S45L"], b["B45L"], b["M1C"]
    SQ, SA, IVQ, Q2T = b["SQ"], b["SA"], b["IVQ"], b["Q2T"]
    st = b["st"]
    psA, psB = b["psA"], b["psB"]
    XRH = [XR[:, 0:DM], XR[:, DM:2 * DM]]
    QHB3 = QHB[:, :].rearrange("p (h t) -> p h t", t=DM)
    wq_tok = b["tWq"]
    init_toks = b["tInit"]
    last_store = None

    for bb in range(2):
        p.barrier()
        if bb == 1:
            def ld_wq2(e):
                ins = []
                for h in range(H):
                    for kc in range(16):
                        ins.append(e.dma_start(
                            out=WQO[:, (kc * 16 + h) * 128:(kc * 16 + h + 1) * 128],
                            in_=io["Wqkv"][kc * 128:(kc + 1) * 128,
                                           h * 384:h * 384 + 128]))
                return ins
            ld_wq2.ndma = H * 16
            if os.environ.get("SKIPW"):
                ld_wq2 = lambda e: [e.dma_start(out=WQO[:, 0:128],
                                                in_=io["Wqkv"][0:128, 0:128])]
                ld_wq2.ndma = 1
            wq_tok = p.step("gpsimd", [], ld_wq2, sem="dG", dma=True)

        xr_free = [None, None]
        st_free = None
        psA_free = init_toks[0] if bb == 0 else None
        l1t_mm = None
        sct_evac = None
        sil_free = [None, None]
        ivq_read = None
        for s in range(4):
            ev_last = None
            for g in range(4):
                gg = s * 4 + g
                half = gg % 2
                xr = XRH[half]
                w = ([init_toks[0]] if (bb == 0 and gg == 0) else []) + \
                    ([xr_free[half]] if xr_free[half] else [])
                tX = dma_step(p, "sync", "dS", w,
                              [(xr, io["x"][bb * 2048 + gg * 128:
                                            bb * 2048 + (gg + 1) * 128, :])])
                wq2 = [tX] + ([psA_free] if psA_free else []) + \
                    ([st_free] if st_free else [])
                tDmp = p.step("scalar", wq2,
                              lambda e, xr=xr: e.activation(
                                  out=psA[:, :], in_=xr, func=AF.Square),
                              sem="cA")
                tQ = p.step("vector", [tDmp] + ([st_free] if st_free else []),
                            lambda e: e.reduce_sum(out=st["ssq"][:, :],
                                                   in_=psA[:, :], axis=AX.X),
                            sem="cV")
                tS = p.step("vector", [tX] + ([st_free] if st_free else []),
                            lambda e, xr=xr: e.reduce_sum(
                                out=st["ssum"][:, :], in_=xr, axis=AX.X),
                            sem="cV")

                def comb1(e):
                    e.tensor_scalar(out=st["negmu"][:, :], in0=st["ssum"][:, :],
                                    scalar1=-1.0 / DM, scalar2=None,
                                    op0=ALU.mult)
                    return e.tensor_scalar(out=st["v1"][:, :],
                                           in0=st["ssq"][:, :],
                                           scalar1=1.0 / DM, scalar2=None,
                                           op0=ALU.mult)
                tC1 = p.step("vector", [tQ, tS], comb1, sem="cV")
                tC2 = p.step("vector", [tC1],
                             lambda e: e.tensor_tensor(
                                 out=st["v2"][:, :], in0=st["negmu"][:, :],
                                 in1=st["negmu"][:, :], op=ALU.mult),
                             sem="cV")
                tC3 = p.step("vector", [tC2],
                             lambda e: e.tensor_tensor(
                                 out=st["v1"][:, :], in0=st["v1"][:, :],
                                 in1=st["v2"][:, :], op=ALU.subtract),
                             sem="cV")
                tC4 = p.step("scalar", [tC3],
                             lambda e: e.activation(out=st["v2"][:, :],
                                                    in_=st["v1"][:, :],
                                                    func=AF.Sqrt,
                                                    bias=EPS5[:, :]),
                             sem="cA")
                tI = p.step("vector", [tC4],
                            lambda e: e.reciprocal(out=st["inv"][:, :],
                                                   in_=st["v2"][:, :]),
                            sem="cV")
                tL = p.step("vector", [tI],
                            lambda e, xr=xr: ts2(e, xr, xr, st["negmu"][:, :],
                                                 st["inv"][:, :], ALU.add,
                                                 ALU.mult),
                            sem="cV")
                st_free = tL
                tT_last = None
                for bi in range(4):
                    def trq(e, bi=bi, xr=xr):
                        last = None
                        for jj in range(4):
                            kc = bi * 4 + jj
                            last = e.transpose(psA[:, jj * 512:jj * 512 + 128],
                                               xr[:, kc * 128:(kc + 1) * 128],
                                               EYE[:])
                        return last
                    wts = [tL] + ([ev_last] if ev_last else []) + \
                        ([l1t_mm] if (l1t_mm and g == 0 and bi == 0) else [])
                    tT = p.step("tensor", wts, trq, sem="cT")
                    tT_last = tT

                    def evq(e, bi=bi, g=g):
                        last = None
                        for jj in range(4):
                            kc = bi * 4 + jj
                            last = e.tensor_copy(
                                out=L1T[:, kc * 512 + g * 128:
                                        kc * 512 + (g + 1) * 128],
                                in_=psA[:, jj * 512:jj * 512 + 128])
                        return last
                    tE = p.step("vector", [tT], evq, sem="cV")
                    ev_last = tE
                psA_free = ev_last
                xr_free[half] = tT_last
            def mmsct(e):
                last = None
                for kc in range(16):
                    last = e.matmul(psB[0:16, 1536:2048],
                                    WsT[:, kc * 16:(kc + 1) * 16],
                                    L1T[:, kc * 512:(kc + 1) * 512],
                                    start=(kc == 0), stop=(kc == 15))
                return last
            tSCm = p.step("tensor", [ev_last] + ([sct_evac] if sct_evac else []),
                          mmsct, sem="cT")
            tSCe = p.step("scalar", [tSCm],
                          lambda e, s=s: e.activation(
                              out=SCTB[0:16, s * 512:(s + 1) * 512],
                              in_=psB[0:16, 1536:2048], func=AF.Silu,
                              bias=BSV[:, :]),
                          sem="cA")
            sct_evac = tSCe
            zb_prev = None
            sq_prev = None
            for h in range(H):
                bank = h % 2
                col = bank * 512

                def mmq(e, h=h, col=col):
                    last = None
                    for kc in range(16):
                        last = e.matmul(
                            psB[:, col:col + 512],
                            WQO[:, (kc * 16 + h) * 128:(kc * 16 + h + 1) * 128],
                            L1T[:, kc * 512:(kc + 1) * 512],
                            start=(kc == 0), stop=(kc == 15))
                    return last
                wts = [ev_last, wq_tok] + \
                    ([sil_free[bank]] if sil_free[bank] else [])
                tM = p.step("tensor", wts, mmq, sem="cT")
                l1t_mm = tM
                tSil = p.step("scalar", [tM],
                              lambda e, h=h, s=s, col=col: e.activation(
                                  out=QHB[:, h * 2048 + s * 512:
                                          h * 2048 + (s + 1) * 512],
                                  in_=psB[:, col:col + 512], func=AF.Silu),
                              sem="cA")
                sil_free[bank] = tSil
                wsq = [tSil] + ([zb_prev] if zb_prev else [])
                tSq = p.step("scalar", wsq,
                             lambda e, h=h, s=s: e.activation(
                                 out=Q2T[:, :],
                                 in_=QHB[:, h * 2048 + s * 512:
                                         h * 2048 + (s + 1) * 512],
                                 func=AF.Square),
                             sem="cA")
                wzb = [tSq] + ([ivq_read] if (h == 0 and ivq_read) else [])
                tZb = p.step("tensor", wzb,
                             lambda e, h=h: e.matmul(
                                 psB[0:16, 1024:1536], Zb[:, 15 - h:31 - h],
                                 Q2T[:, :], start=(h == 0), stop=(h == 15)),
                             sem="cT")
                zb_prev = tZb
            tN1 = p.step("scalar", [zb_prev],
                         lambda e: e.activation(out=SQ[:, :],
                                                in_=psB[0:16, 1024:1536],
                                                func=AF.Sqrt,
                                                bias=EPSQ[0:16, :]),
                         sem="cA")
            ivq_read = tN1
            t_1 = p.step("vector", [tN1],
                         lambda e: e.tensor_scalar(out=SQ[:, :], in0=SQ[:, :],
                                                   scalar1=1e-5, scalar2=None,
                                                   op0=ALU.add),
                         sem="cV")
            t_2 = p.step("vector", [t_1],
                         lambda e: e.reciprocal(out=SA[:, :], in_=SQ[:, :]),
                         sem="cV")
            tIVQ = p.step("vector", [t_2],
                          lambda e: e.tensor_copy(out=IVQ[:, :], in_=SA[:, :]),
                          sem="cV")
            bc_prev = [None, None]
            for h in range(H):
                bank = h % 2
                col = bank * 512
                wts = [tIVQ] + ([bc_prev[bank]] if bc_prev[bank] else [])
                tBC = p.step("tensor", wts,
                             lambda e, h=h, col=col: e.matmul(
                                 psB[:, col:col + 512],
                                 RS1[:, h * 128:(h + 1) * 128], IVQ[:, :],
                                 start=True, stop=True),
                             sem="cT")
                tTT = p.step("vector", [tBC],
                             lambda e, h=h, s=s, col=col: e.tensor_tensor(
                                 out=QHB[:, h * 2048 + s * 512:
                                         h * 2048 + (s + 1) * 512],
                                 in0=QHB[:, h * 2048 + s * 512:
                                         h * 2048 + (s + 1) * 512],
                                 in1=psB[:, col:col + 512], op=ALU.mult),
                             sem="cV")
                bc_prev[bank] = tTT
        SCTBr = SCTB[:, :].rearrange("p (r s) -> p s r", s=16)
        tPm = None
        for h in range(H):
            w = [sct_evac] if h == 0 else [tPm]
            tPm = p.step("vector", w,
                         lambda e, h=h: e.tensor_copy(
                             out=SCTH[0:16, h * 128:(h + 1) * 128],
                             in_=SCTBr[0:16, h, :]),
                         sem="cV")
        if bb == 0:
            dbg_store(p, io, "qhb", p.tok("cV"), QHB[:, 0:4096])
            dbg_store(p, io, "sctb", tPm, SCTB[0:16, :])
            dbg_store(p, io, "scth", tPm, SCTH[0:16, :])
        if STOPAT == "phasea":
            return None
        p.barrier()

        def ld_wo(e):
            ins = []
            for j in range(16):
                ins.append(e.dma_start(
                    out=WQO[:, j * 2048:(j + 1) * 2048],
                    in_=io["Wo"][j * 128:(j + 1) * 128, :]))
            return ins
        ld_wo.ndma = 16
        wo_tok = p.step("gpsimd", [], ld_wo, sem="dG", dma=True)

        g_free = None
        gt_free = None
        o_free = None
        of_free = [None] * 4
        o2_free = None
        s_free = None
        bci_free = None
        bcn_free = None
        bcs_free = None
        sst_free = None
        for j in range(16):
            g0_toks = []
            of_toks = []
            s1_tok = None
            s2_tok = None
            for mu in range(4):
                mv = QHB3[:, 4 * mu:4 * mu + 4, j * 128:(j + 1) * 128]
                wts = ([g_free] if g_free else [])
                tG0 = p.step("tensor", wts,
                             lambda e, j=j, mv=mv: e.matmul(
                                 psA[:, 0:512], W0T[:, j * 128:(j + 1) * 128],
                                 mv, start=True, stop=True),
                             sem="cT")
                tG2 = p.step("tensor", [],
                             lambda e, j=j, mv=mv: e.matmul(
                                 psA[:, 512:1024], W2T[:, j * 128:(j + 1) * 128],
                                 mv, start=True, stop=True),
                             sem="cT")
                g0_toks.append((tG0, tG2))
                tGT0 = p.step("scalar", [tG0],
                              lambda e: e.activation(out=GT0[:, :],
                                                     in_=psA[:, 0:512],
                                                     func=AF.Silu),
                              sem="cA")
                tGT = p.step("vector",
                             [tGT0, tG2] + ([gt_free] if gt_free else []),
                             lambda e: e.tensor_tensor(
                                 out=GT[:, :], in0=GT0[:, :],
                                 in1=psA[:, 512:1024], op=ALU.mult),
                             sem="cV")
                g_free = tGT
                wts = [tGT] + ([o_free] if o_free else [])
                tO = p.step("tensor", wts,
                            lambda e, j=j: e.matmul(
                                psA[:, 1024:1536],
                                W1T[:, j * 128:(j + 1) * 128], GT[:, :],
                                start=True, stop=True),
                            sem="cT")
                gt_free = tO
                tOF = p.step("scalar",
                             [tO] + ([of_free[mu]] if of_free[mu] else []),
                             lambda e, mu=mu: e.activation(
                                 out=OF[:, mu * 512:(mu + 1) * 512],
                                 in_=psA[:, 1024:1536], func=AF.Copy),
                             sem="cA")
                tO2 = p.step("scalar",
                             [tOF] + ([o2_free] if o2_free else []),
                             lambda e: e.activation(out=O2[:, :],
                                                    in_=psA[:, 1024:1536],
                                                    func=AF.Square),
                             sem="cA")
                o_free = tO2
                of_toks.append(tOF)
                wts = [tOF] + ([s_free] if (mu == 0 and s_free) else [])
                s1_tok = p.step("tensor", wts + ([s1_tok] if s1_tok else []),
                                lambda e, mu=mu: e.matmul(
                                    psB[0:4, 0:512],
                                    ONES4[:, mu * 4:(mu + 1) * 4],
                                    OF[:, mu * 512:(mu + 1) * 512],
                                    start=(mu == 0), stop=(mu == 3)),
                                sem="cT")
                s2_tok = p.step("tensor", [tO2] + ([s2_tok] if s2_tok else []),
                                lambda e, mu=mu: e.matmul(
                                    psB[0:4, 512:1024],
                                    ONES4[:, mu * 4:(mu + 1) * 4], O2[:, :],
                                    start=(mu == 0), stop=(mu == 3)),
                                sem="cT")
                o2_free = s2_tok
            wts = [s1_tok] + ([sst_free] if sst_free else [])
            tNMU = p.step("vector", wts,
                          lambda e: e.tensor_scalar(
                              out=SSTA[:, :], in0=psB[0:4, 0:512],
                              scalar1=-1.0 / 128, scalar2=None, op0=ALU.mult),
                          sem="cV")
            tMU2 = p.step("vector", [tNMU],
                          lambda e: e.tensor_tensor(
                              out=SSTB[:, :], in0=SSTA[:, :],
                              in1=SSTA[:, :], op=ALU.mult),
                          sem="cV")
            tV = p.step("vector", [tMU2, s2_tok],
                        lambda e: e.scalar_tensor_tensor(
                            out=SSTB[:, :], in0=psB[0:4, 512:1024],
                            scalar=1.0 / 128, in1=SSTB[:, :],
                            op0=ALU.mult, op1=ALU.subtract),
                        sem="cV")
            s_free = tV
            tSD = p.step("scalar", [tV],
                         lambda e: e.activation(out=SQ[0:4, :],
                                                in_=SSTB[:, :],
                                                func=AF.Sqrt,
                                                bias=EPS6[0:4, :]),
                         sem="cA")
            tIV = p.step("vector", [tSD],
                         lambda e: e.reciprocal(out=SA[0:4, :],
                                                in_=SQ[0:4, :]),
                         sem="cV")
            tIVb = p.step("vector", [tIV],
                          lambda e: e.tensor_copy(out=SST[:, 0:512],
                                                  in_=SA[0:4, :]),
                          sem="cV")
            tNM = p.step("vector", [tIVb],
                         lambda e: e.tensor_tensor(
                             out=SST[:, 512:1024], in0=SSTA[:, :],
                             in1=SA[0:4, :], op=ALU.mult),
                         sem="cV")
            tt4_toks = []
            for mu in range(4):
                wts = [tNM] + ([bci_free] if bci_free else [])
                tBCI = p.step("tensor", wts,
                              lambda e, mu=mu: e.matmul(
                                  psB[:, 1024:1536],
                                  RSB4[:, mu * 128:(mu + 1) * 128],
                                  SST[0:4, 0:512], start=True, stop=True),
                              sem="cT")
                wts = [tNM] + ([bcn_free] if bcn_free else [])
                tBCN = p.step("tensor", wts,
                              lambda e, mu=mu: e.matmul(
                                  psB[:, 1536:2048],
                                  RSB4[:, mu * 128:(mu + 1) * 128],
                                  SST[0:4, 512:1024], start=True, stop=True),
                              sem="cT")
                tT1 = p.step("vector", [tBCI],
                             lambda e, mu=mu: e.tensor_tensor(
                                 out=TB1[:, :],
                                 in0=OF[:, mu * 512:(mu + 1) * 512],
                                 in1=psB[:, 1024:1536], op=ALU.mult),
                             sem="cV")
                bci_free = tT1
                of_free[mu] = tT1
                tT2 = p.step("vector", [tT1, tBCN],
                             lambda e: e.tensor_tensor(
                                 out=TB1[:, :], in0=TB1[:, :],
                                 in1=psB[:, 1536:2048], op=ALU.add),
                             sem="cV")
                bcn_free = tT2
                tT3 = p.step("scalar", [tT2],
                             lambda e: e.activation(out=TB2[:, :],
                                                    in_=TB1[:, :],
                                                    func=AF.Identity,
                                                    scale=LNG[:, :],
                                                    bias=LNB[:, :]),
                             sem="cA")
                wts = ([tPm] if (j == 0 and mu == 0) else []) + \
                    ([bcs_free] if bcs_free else [])
                tBCS = p.step("tensor", wts,
                              lambda e, j=j, mu=mu: e.matmul(
                                  psA[:, 1536:2048],
                                  RS1[:, j * 128:(j + 1) * 128],
                                  SCTH[:, mu * 512:(mu + 1) * 512],
                                  start=True, stop=True),
                              sem="cT")
                TB23 = TB2[:, :].rearrange("p (i t) -> p i t", t=128)
                PS3 = psA[:, 1536:2048].rearrange("p (i t) -> p i t", t=128)
                tT4 = p.step("vector",
                             [tT3, tBCS, g0_toks[mu][0], g0_toks[mu][1]],
                             lambda e, j=j, mu=mu, TB23=TB23, PS3=PS3: e.tensor_tensor(
                                 out=QHB3[:, 4 * mu:4 * mu + 4,
                                          j * 128:(j + 1) * 128],
                                 in0=TB23, in1=PS3, op=ALU.mult),
                             sem="cV")
                bcs_free = tT4
                tt4_toks.append(tT4)
            sst_free = tt4_toks[-1]
            if bb == 0 and j == 0:
                dbg_store(p, io, "oj", of_toks[-1], OF[:, :])
                dbg_store(p, io, "ojf", tt4_toks[-1], QHB[:, 0:2048])
        if STOPAT == "phaseb":
            return None
        p.barrier()

        xr_free = [None, None]
        ev_free = None
        y_mask_free = None
        rq_evac = None
        x1_free = None
        for m in range(16):
            half = m % 2
            xrm = XRH[half]
            xv = io["x"][:, :].rearrange("(b r s) d -> b s r d", b=2, s=16)
            tXr = dma_step(p, "sync", "dS",
                           ([xr_free[half]] if xr_free[half] else []),
                           [(xrm, xv[bb, m])])
            def mmwo(e, m=m):
                last = None
                for j in range(16):
                    for nu in range(4):
                        last = e.matmul(
                            psA[:, nu * 512:(nu + 1) * 512],
                            QHB[:, m * 2048 + j * 128:m * 2048 + (j + 1) * 128],
                            WQO[:, j * 2048 + nu * 512:j * 2048 + (nu + 1) * 512],
                            start=(j == 0), stop=(j == 15))
                return last
            wts = [wo_tok] + ([ev_free] if ev_free else [])
            tWmm = p.step("tensor", wts, mmwo, sem="cT")
            ev_toks = []
            for nu in range(4):
                wev = [tWmm] + ([y_mask_free] if (nu == 0 and y_mask_free)
                                else [])
                tEv = p.step("scalar", wev,
                             lambda e, nu=nu: e.activation(
                                 out=Y[:, nu * 512:(nu + 1) * 512],
                                 in_=psA[:, nu * 512:nu * 512 + 512],
                                 func=AF.Copy),
                             sem="cA")
                ev_toks.append(tEv)
            ev_free = ev_toks[-1]
            if bb == 0 and m == 0:
                dbg_store(p, io, "y", ev_toks[-1], Y[:, :])
            wts = list(ev_toks) + ([x1_free] if x1_free else [])
            tIy = ln_stats(p, Y[:, :], DM, X1[:, :], EPS5[:, :],
                           st["negmu"][:, :], st["inv"][:, :], st,
                           wts, f"lny_{bb}_{m}")
            tNy = p.step("vector", [tIy],
                         lambda e: ts2(e, Y[:, :], Y[:, :], st["negmu"][:, :],
                                       st["inv"][:, :], ALU.add, ALU.mult),
                         sem="cV")
            tX1 = p.step("vector", [tNy, tXr],
                         lambda e, xrm=xrm: e.tensor_tensor(
                             out=X1[:, :], in0=xrm, in1=Y[:, :], op=ALU.add),
                         sem="cV")
            slot = bb * 16 + m
            tSt1 = dma_step(p, "scalar", "dA", [tX1],
                            [(io["x1s"][slot * 128:(slot + 1) * 128, :],
                              X1[:, :])])
            tI2 = ln_stats(p, X1[:, :], DM, Y[:, :], EPS5[:, :],
                           st["negmu"][:, :], st["inv"][:, :], st,
                           [tX1], f"ln2_{bb}_{m}")
            tN2 = p.step("vector", [tI2],
                         lambda e: ts2(e, Y[:, :], X1[:, :], st["negmu"][:, :],
                                       st["inv"][:, :], ALU.add, ALU.mult),
                         sem="cV")
            tQa = p.step("vector", [tN2],
                         lambda e: e.reduce_max(out=st["m1"][:, :], in_=Y[:, :],
                                                axis=AX.X,
                                                apply_absolute_value=True),
                         sem="cV")
            tQb = p.step("vector", [tQa],
                         lambda e, m=m: e.tensor_scalar(
                             out=M1C[:, m:m + 1], in0=st["m1"][:, :],
                             scalar1=1e-5, scalar2=None, op0=ALU.max),
                         sem="cV")
            tQc = p.step("vector", [tQb],
                         lambda e, m=m: e.reciprocal(
                             out=st["v1"][:, :], in_=M1C[:, m:m + 1]),
                         sem="cV")
            tQs = p.step("vector", [tQc],
                         lambda e: e.tensor_scalar(out=st["qs"][:, :],
                                                   in0=st["v1"][:, :],
                                                   scalar1=127.0, scalar2=None,
                                                   op0=ALU.mult),
                         sem="cV")
            tZa = p.step("vector", [tQs],
                         lambda e: ts2(e, Z0[:, :], Y[:, :], st["qs"][:, :],
                                       127.0, ALU.mult, ALU.add),
                         sem="cV")
            tZ = p.step("vector", [tZa],
                        lambda e, xrm=xrm: e.tensor_copy(out=xrm, in_=Z0[:, :]),
                        sem="cV")
            tLad = ladder(p, xrm, Y[:, :], DM, [tZ], f"lad_{bb}_{m}")
            y_mask_free = tLad
            tRa = p.step("vector", [tLad, tSt1],
                         lambda e, xrm=xrm: e.tensor_tensor(
                             out=X1[:, :], in0=Z0[:, :], in1=xrm,
                             op=ALU.subtract),
                         sem="cV")
            tR = p.step("vector", [tRa],
                        lambda e: e.tensor_scalar(out=X1[:, :], in0=X1[:, :],
                                                  scalar1=-127.0, scalar2=None,
                                                  op0=ALU.add),
                        sem="cV")
            xr_free[half] = tRa
            tT = None
            for bi in range(4):
                def trr(e, bi=bi):
                    last = None
                    for jj in range(4):
                        kc = bi * 4 + jj
                        last = e.transpose(psB[:, jj * 512:jj * 512 + 128],
                                           X1[:, kc * 128:(kc + 1) * 128],
                                           EYE[:])
                    return last
                wts = [tR] + ([rq_evac] if rq_evac else [])
                tT = p.step("tensor", wts, trr, sem="cT")

                def evr(e, bi=bi):
                    last = None
                    for jj in range(4):
                        kc = bi * 4 + jj
                        last = e.tensor_copy(
                            out=RQT[:, kc * 128:(kc + 1) * 128],
                            in_=psB[:, jj * 512:jj * 512 + 128])
                    return last
                rq_evac = p.step("vector", [tT], evr, sem="cV")
            x1_free = tT

            def strq(e, slot=slot):
                src = RQT[:, :].rearrange("q (kc s) -> q kc s", s=128)
                dst = io["rqTs"][:, :].rearrange("(kc q) n -> q kc n", q=128)[
                    :, :, slot * 128:(slot + 1) * 128]
                return [e.dma_start(out=dst, in_=src)]
            strq.ndma = 1
            tSrq = p.step("scalar", [rq_evac], strq, sem="dA", dma=True)
            rq_evac = tSrq

            if m == 15:
                def stcs(e, bb=bb):
                    return [e.dma_start(out=io["cs1"][:, bb * 16:(bb + 1) * 16],
                                        in_=M1C[:, :])]
                stcs.ndma = 1
                tScs = p.step("scalar", [tQb], stcs, sem="dA", dma=True)
            last_store = tSrq
    return last_store



def p2_build(p, io, mk, psA, THR, MAB, RS0F, start_waits):
    nc = p.nc
    WB = [mk("p2_wb0", [128, FF], F32), mk("p2_wb1", [128, FF], F32)]
    TRN = mk("p2_trn", [128, FF], F32)
    TTB = mk("p2_ttb", [128, 64 * 128], BF16)
    ACC = mk("p2_acc", [128, 2], F32)
    VAq = mk("p2_va", [128, 2], F32)
    ONE = mk("p2_one", [128, 1], F32)
    MP = mk("p2_mp", [16, 2], F32)
    SCL = mk("p2_scl", [1, 8], F32)
    S45 = mk("p2_s45", [128, 1], F32)
    B45 = [mk("p2_b45a", [128, 1], F32), mk("p2_b45b", [128, 1], F32)]

    t0 = p.step("vector", start_waits,
                lambda e: [e.memset(ONE[:, :], 1.0), e.memset(MP[0:16, :], 0.0),
                           e.memset(S45[:, :], -POW45),
                           e.memset(ACC[:, :], 0.0)][-1],
                sem="cV")
    tR = start_waits[0] if start_waits else None

    toks = {}
    for wi, (key, nch, ncol, nsum) in enumerate(
            (("f_w1", 64, DM, FF * DM), ("f_w2", 16, FF, DM * FF))):
        tprev = [t0] + ([tR] if tR else [])
        tload = [None, None]
        tacc = t0
        for c in range(nch):
            buf = WB[c % 2]
            w = [tprev[0]] if c < 2 else [toks[("a", wi, c - 2)]]
            tload[c % 2] = dma_step(p, "sync", "dS", w + [t0],
                                    [(buf[:, 0:ncol],
                                      io[key][c * 128:(c + 1) * 128, :])])
            tRd = p.step("vector", [tload[c % 2]],
                         lambda e, buf=buf, c=c, ncol=ncol: e.reduce_sum(
                             out=VAq[:, c % 2:c % 2 + 1], in_=buf[:, 0:ncol],
                             axis=AX.X, apply_absolute_value=True),
                         sem="cV")
            tacc = p.step("vector", [tRd, tacc],
                          lambda e, c=c, wi=wi: e.tensor_tensor(
                              out=ACC[:, wi:wi + 1], in0=ACC[:, wi:wi + 1],
                              in1=VAq[:, c % 2:c % 2 + 1], op=ALU.add),
                          sem="cV")
            toks[("a", wi, c)] = tacc
        tPS = p.step("tensor", [tacc],
                     lambda e, wi=wi: e.matmul(psA[0:1, 0:1], ONE[:, 0:1],
                                               ACC[:, wi:wi + 1],
                                               start=True, stop=True),
                     sem="cT")
        tM1 = p.step("vector", [tPS],
                     lambda e, wi=wi, nsum=nsum: e.tensor_scalar(
                         out=SCL[0:1, 4 * wi:4 * wi + 1], in0=psA[0:1, 0:1],
                         scalar1=1.0 / nsum, scalar2=None, op0=ALU.mult),
                     sem="cV")
        tM2 = p.step("vector", [tM1],
                     lambda e, wi=wi: e.tensor_scalar(
                         out=SCL[0:1, 4 * wi + 1:4 * wi + 2],
                         in0=SCL[0:1, 4 * wi:4 * wi + 1],
                         scalar1=1e-5, scalar2=None, op0=ALU.max),
                     sem="cV")
        tM3 = p.step("vector", [tM2],
                     lambda e, wi=wi: e.tensor_scalar(
                         out=SCL[0:1, 4 * wi + 2:4 * wi + 3],
                         in0=SCL[0:1, 4 * wi + 1:4 * wi + 2],
                         scalar1=0.5, scalar2=None, op0=ALU.mult),
                     sem="cV")
        tP1 = p.step("vector", [tM3],
                     lambda e, wi=wi: e.tensor_copy(
                         out=MP[0:1, 0:1],
                         in_=SCL[0:1, 4 * wi + 2:4 * wi + 3]),
                     sem="cV")
        tP1b = p.step("vector", [tP1],
                      lambda e, wi=wi: e.tensor_copy(
                          out=MP[0:1, 1:2],
                          in_=SCL[0:1, 4 * wi + 1:4 * wi + 2]),
                      sem="cV")
        tBC = p.step("tensor", [tP1b],
                     lambda e: e.matmul(psA[:, 0:2], RS0F[:, :], MP[:, :],
                                        start=True, stop=True),
                     sem="cT")
        tTH = p.step("vector", [tBC],
                     lambda e, wi=wi: [e.tensor_copy(out=THR[wi][:, :],
                                                     in_=psA[:, 0:1]),
                                       e.tensor_copy(out=MAB[wi][:, :],
                                                     in_=psA[:, 1:2])][-1],
                     sem="cV")
        tB45 = p.step("vector", [tTH],
                      lambda e, wi=wi: e.tensor_scalar(
                          out=B45[wi][:, :], in0=THR[wi][:, :], scalar1=-POW45,
                          scalar2=None, op0=ALU.mult),
                      sem="cV")

        ev_tok = None
        st_tok = None
        last_tT = None
        for c in range(nch):
            buf = WB[c % 2]
            w = [tB45] if c < 2 else [toks[("b", wi, c - 2)]]
            if st_tok is not None:
                w = w + [st_tok]
            tld = dma_step(p, "sync", "dS", w,
                           [(buf[:, 0:ncol], io[key][c * 128:(c + 1) * 128, :])])
            tSg = p.step("scalar", [tld] + ([last_tT] if last_tT else []),
                         lambda e, buf=buf, ncol=ncol, wi=wi: e.activation(
                             out=TRN[:, 0:ncol], in_=buf[:, 0:ncol],
                             func=AF.Sigmoid, scale=S45[:, :], bias=B45[wi][:, :]),
                         sem="cA")
            tTn = p.step("vector", [tSg],
                         lambda e, buf=buf, ncol=ncol, wi=wi: e.scalar_tensor_tensor(
                             out=TRN[:, 0:ncol], in0=buf[:, 0:ncol],
                             scalar=THR[wi][:, :], in1=TRN[:, 0:ncol],
                             op0=ALU.is_ge, op1=ALU.subtract),
                         sem="cV")
            toks[("b", wi, c)] = tTn
            nbl = ncol // 128
            for bi in range(nbl // 4):
                def trp(e, bi=bi):
                    last = None
                    for j in range(4):
                        blk = bi * 4 + j
                        last = e.transpose(psA[:, j * 512:j * 512 + 128],
                                           TRN[:, blk * 128:(blk + 1) * 128],
                                           b_eye(p)[:, :])
                    return last
                wts = [tTn] + ([ev_tok] if ev_tok else [])
                tT = p.step("tensor", wts, trp, sem="cT")
                last_tT = tT

                def evt(e, bi=bi):
                    last = None
                    for j in range(4):
                        blk = bi * 4 + j
                        last = e.tensor_copy(
                            out=TTB[:, blk * 128:(blk + 1) * 128],
                            in_=psA[:, j * 512:j * 512 + 128])
                    return last
                ev_tok = p.step("vector", [tT], evt, sem="cV")
            okey = "wq1t" if wi == 0 else "wq2t"
            ostride = FF if wi == 0 else DM

            def sto(e, c=c, nbl=nbl, okey=okey, ostride=ostride):
                src = TTB[:, :].rearrange("q (dc j) -> q dc j", j=128)[:, 0:nbl, :]
                dst = io[okey][:, :].rearrange("(dc q) n -> q dc n", q=128)[
                    :, 0:nbl, c * 128:(c + 1) * 128]
                return [e.dma_start(out=dst, in_=src)]
            sto.ndma = 1
            wst = [ev_tok] + ([st_tok] if st_tok else [])
            st_tok = p.step("scalar", wst, sto, sem="dA", dma=True)
            ev_tok = st_tok
        toks[("done", wi)] = st_tok
    return dict(SCL=SCL,
                done=[toks[("done", 0)], toks[("done", 1)],
                      p.tok("cV"), p.tok("cT"), p.tok("cA")])


_EYE_CACHE = {}


def b_eye(p):
    return _EYE_CACHE[id(p)]


def set_eye(p, eye_handle):
    _EYE_CACHE[id(p)] = eye_handle



def p3_build(p, io, mk, psA, MAB1, RS0B, SG, SG2, CS, CB, start_waits):
    nc = p.nc
    NSEC = 8
    FS = FF // NSEC
    W1S = [mk("p3_w1s0", [128, 16 * FS], BF16), mk("p3_w1s1", [128, 16 * FS], BF16)]
    RQS = [mk("p3_rqs0", [128, 16 * 128], BF16), mk("p3_rqs1", [128, 16 * 128], BF16)]
    B1M = mk("p3_b1m", [16, 512], BF16)
    B1BC = mk("p3_b1bc", [128, FF], BF16)
    HP = mk("p3_hp", [128, 512], F32)
    GT = mk("p3_gt", [128, FS], BF16)
    GSQ = mk("p3_gsq", [128, FS], BF16)
    V1 = mk("p3_v1", [128, 2], F32)

    t0 = p.step("vector", start_waits,
                lambda e: [e.memset(B1M[0:16, :], 0.0), e.memset(SG[:, :], 0.0),
                           e.memset(SG2[:, :], 0.0)][-1], sem="cV")
    tCS = dma_step(p, "sync", "dS", start_waits, [(CS[:, :], io["cs1"][:, :])])
    tCb0 = p.step("vector", start_waits + [tCS],
                  lambda e: e.tensor_scalar(out=V1[:, 0:1], in0=MAB1[:, :],
                                            scalar1=1.0 / 127, scalar2=None,
                                            op0=ALU.mult), sem="cV")
    tCb = p.step("vector", [tCb0],
                 lambda e: e.tensor_scalar(out=CB[:, :], in0=CS[:, :],
                                           scalar1=V1[:, 0:1], scalar2=None,
                                           op0=ALU.mult), sem="cV")
    tprev = [t0]
    for i in range(16):
        tld = p.step("gpsimd", tprev,
                     lambda e, i=i: [e.dma_start(
                         out=B1M[0:1, :],
                         in_=io["b1r"][:, i * 512:(i + 1) * 512])],
                     sem="dG", dma=True)
        tm = p.step("tensor", [tld] + tprev,
                    lambda e: e.matmul(psA[:, 0:512], RS0B, B1M[:, :],
                                       start=True, stop=True),
                    sem="cT")
        tev = p.step("vector", [tm],
                     lambda e, i=i: e.tensor_copy(
                         out=B1BC[:, i * 512:(i + 1) * 512], in_=psA[:, 0:512]),
                     sem="cV")
        tprev = [tev]
    tB1BC = tprev[0]
    if io.get("dbg3") is not None:
        dma_step(p, "gpsimd", "dG", [tB1BC], [(io["dbg3"][:, :], B1BC[:, :])])

    sg_tok = {}
    ld_w = [None, None]
    ld_r = {}
    st_g = None
    ev_tok = [tB1BC, tB1BC]
    hp_free = None
    for sec in range(NSEC):
        wbuf = W1S[sec % 2]
        w = [tB1BC, tCb] if sec < 2 else [sg_tok[(sec - 2, NSB - 1, "g")]]

        def ldw(e, sec=sec, wbuf=wbuf):
            src = io["wq1t"][:, :].rearrange("(kc q) n -> q kc n", q=128)[
                :, :, sec * FS:(sec + 1) * FS]
            return [e.dma_start(out=wbuf[:, :].rearrange("q (kc n) -> q kc n", n=FS),
                                in_=src)]
        ldw.ndma = 1
        ld_w[sec % 2] = p.step("sync", w, ldw, sem="dS", dma=True)
        tW = ld_w[sec % 2]
        for sb in range(NSB):
            rbuf = RQS[sb % 2]
            g = sec * NSB + sb
            w = []
            if g >= 2:
                ps, pb = divmod(g - 2, NSB)
                w = [sg_tok[(ps, pb, "mm")]]

            def ldr(e, sb=sb, rbuf=rbuf):
                src = io["rqTs"][:, :].rearrange("(kc q) n -> q kc n", q=128)[
                    :, :, sb * 128:(sb + 1) * 128]
                return [e.dma_start(
                    out=rbuf[:, :].rearrange("q (kc n) -> q kc n", n=128),
                    in_=src)]
            ldr.ndma = 1
            tR = p.step("scalar", w, ldr, sem="dA", dma=True)
            ld_r[(sec, sb)] = tR

            gelu_toks = []
            for fs in range(2):
                bank = fs
                def mmh(e, fs=fs, bank=bank, rbuf=rbuf, wbuf=wbuf):
                    last = None
                    for kc in range(16):
                        last = e.matmul(
                            psA[:, bank * 512:bank * 512 + 512],
                            rbuf[:, kc * 128:(kc + 1) * 128],
                            wbuf[:, kc * FS + fs * 512: kc * FS + (fs + 1) * 512],
                            start=(kc == 0), stop=(kc == 15))
                    return last
                wm = [tR, tW]
                if ev_tok[bank] is not None:
                    wm.append(ev_tok[bank])
                tM = p.step("tensor", wm, mmh, sem="cT")
                tEv = p.step("vector", [tM] + ([hp_free] if hp_free else []),
                             lambda e, fs=fs, bank=bank, sb=sb, sec=sec: e.scalar_tensor_tensor(
                                 out=HP[:, :], in0=psA[:, bank * 512:bank * 512 + 512],
                                 scalar=CB[:, sb:sb + 1],
                                 in1=B1BC[:, sec * FS + fs * 512:sec * FS + (fs + 1) * 512],
                                 op0=ALU.mult, op1=ALU.add),
                             sem="cV")
                ev_tok[bank] = tEv
                if sec == 0 and sb == 0 and fs == 0 and io.get("dbg4") is not None:
                    dma_step(p, "gpsimd", "dG", [tEv], [(io["dbg4"][:, :], HP[:, :])])
                tG = p.step("scalar", [tEv],
                            lambda e, fs=fs: e.activation(
                                out=GT[:, fs * 512:(fs + 1) * 512], in_=HP[:, :],
                                func=AF.Gelu),
                            sem="cA")
                ev_tok[bank] = tG
                hp_free = tG
                gelu_toks.append(tG)
            sg_tok[(sec, sb, "mm")] = gelu_toks[-1]
            def stg(e, sec=sec, sb=sb):
                return [e.dma_start(
                    out=io["gs"][sb * 128:(sb + 1) * 128, sec * FS:(sec + 1) * FS],
                    in_=GT[:, :])]
            stg.ndma = 1
            wst = gelu_toks + ([st_g] if st_g else [])
            st_g = p.step("scalar", wst, stg, sem="dA", dma=True)
            tSq = p.step("scalar", [st_g],
                         lambda e: e.activation(out=GSQ[:, :], in_=GT[:, :],
                                                func=AF.Square),
                         sem="cA")
            wr = [gelu_toks[-1]]
            if (sec - 1, sb, "s2") in sg_tok:
                wr.append(sg_tok[(sec - 1, sb, "s2")])
            tR1 = p.step("vector", wr,
                         lambda e: e.reduce_sum(out=V1[:, 0:1], in_=GT[:, :],
                                                axis=AX.X), sem="cV")
            tA1 = p.step("vector", [tR1],
                         lambda e, sb=sb: e.tensor_tensor(
                             out=SG[:, sb:sb + 1], in0=SG[:, sb:sb + 1],
                             in1=V1[:, 0:1], op=ALU.add), sem="cV")
            tR2 = p.step("vector", [tSq, tA1],
                         lambda e: e.reduce_sum(out=V1[:, 1:2], in_=GSQ[:, :],
                                                axis=AX.X), sem="cV")
            tA2 = p.step("vector", [tR2],
                         lambda e, sb=sb: e.tensor_tensor(
                             out=SG2[:, sb:sb + 1], in0=SG2[:, sb:sb + 1],
                             in1=V1[:, 1:2], op=ALU.add), sem="cV")
            sg_tok[(sec, sb, "s2")] = tA2
            sg_tok[(sec, sb, "g")] = tA2
    done = [st_g, sg_tok[(NSEC - 1, NSB - 1, "s2")], p.tok("cV"), p.tok("cA"),
            p.tok("cT")]
    return dict(done=done)



def ln_statsP(p, x_ap, n, dump_ps, eps_ap, negmu, inv, sc, waits, tag):
    tA = p.step("scalar", waits,
                lambda e: e.activation(out=dump_ps, in_=x_ap, func=AF.Square),
                sem="cA")
    tV = p.step("vector", waits,
                lambda e: e.reduce_sum(out=sc["ssum"][:, :], in_=x_ap, axis=AX.X),
                sem="cV")
    tV = p.step("vector", [tA, tV],
                lambda e: e.reduce_sum(out=sc["ssq"][:, :], in_=dump_ps, axis=AX.X),
                sem="cV")

    def comb1(e):
        e.tensor_scalar(out=negmu, in0=sc["ssum"][:, :], scalar1=-1.0 / n,
                        scalar2=None, op0=ALU.mult)
        return e.tensor_scalar(out=sc["v1"][:, :], in0=sc["ssq"][:, :],
                               scalar1=1.0 / n, scalar2=None, op0=ALU.mult)
    tC1 = p.step("vector", [tV], comb1, sem="cV")
    tC2 = p.step("vector", [tC1],
                 lambda e: e.tensor_tensor(out=sc["v2"][:, :], in0=negmu,
                                           in1=negmu, op=ALU.mult),
                 sem="cV")
    tC = p.step("vector", [tC2],
                lambda e: e.tensor_tensor(out=sc["v1"][:, :], in0=sc["v1"][:, :],
                                          in1=sc["v2"][:, :], op=ALU.subtract),
                sem="cV")
    tS = p.step("scalar", [tC],
                lambda e: e.activation(out=sc["v2"][:, :], in_=sc["v1"][:, :],
                                       func=AF.Sqrt, bias=eps_ap),
                sem="cA")
    tI = p.step("vector", [tS],
                lambda e: e.reciprocal(out=inv, in_=sc["v2"][:, :]),
                sem="cV")
    return (tA, tI)


def p4_build(p, io, mk, psA, psB, EYE, MAB2, RS0B, SG, SG2, M2C, C2S,
             start_waits):
    nc = p.nc
    NG = NSB // 4
    GBUF = mk("p4_g0", [128, 1024], BF16)
    U2F = mk("p4_u2f", [128, FF], F32)
    ZZF = mk("p4_zzf", [128, FF], F32)
    MASKF = mk("p4_maskf", [128, FF], F32)
    RQ2T = mk("p4_rq2t", [128, 4 * 64 * 128], BF16)
    W2S = [mk(f"p4_w2s{i}", [128, 512], BF16) for i in range(4)]
    GBM = mk("p4_gbm", [16, 512], BF16)
    GBC = mk("p4_gbc", [128, FF], BF16)
    BBC = mk("p4_bbc", [128, FF], BF16)
    B2BC = mk("p4_b2bc", [128, DM], BF16)
    X1C = mk("p4_x1c", [128, 512], F32, at="p4_w2s0")
    CB2 = mk("p4_cb2", [128, 1], F32)
    S45P = mk("p4_s45", [128, 1], F32)
    B45P = mk("p4_b45", [128, 8], F32)
    EPS5b = mk("p4_eps5", [128, 1], F32)
    st4 = {k: mk("p4_st_" + k, [128, 1], F32)
           for k in ["nmu", "e2", "v2", "var", "inv", "m2", "qs", "s1", "s2"]}
    NMU4 = [mk(f"p4_nmu{i}", [128, 1], F32) for i in range(4)]
    INV4 = [mk(f"p4_inv{i}", [128, 1], F32) for i in range(4)]
    Y2 = [MASKF[:, i * DM:(i + 1) * DM] for i in range(4)]

    def init0(e):
        e.memset(GBM[0:16, :], 0.0)
        e.memset(S45P[:, :], POW45)
        for k in range(8):
            e.memset(B45P[:, 7 - k:8 - k], -(2.0 ** k - 0.5) * POW45)
        return e.memset(EPS5b[:, :], 1e-5)
    t0 = p.step("vector", start_waits, init0, sem="cV")
    tCB = p.step("vector", start_waits,
                 lambda e: e.tensor_scalar(out=CB2[:, :], in0=MAB2[:, :],
                                           scalar1=1.0 / 127, scalar2=None,
                                           op0=ALU.mult), sem="cV")
    tprev = t0
    for key, dst, n in (("gmr", GBC, FF), ("ber", BBC, FF), ("b2r", B2BC, DM)):
        for i in range(n // 512):
            tL = p.step("gpsimd", [tprev],
                        lambda e, key=key, i=i: [e.dma_start(
                            out=GBM[0:1, :],
                            in_=io[key][:, i * 512:(i + 1) * 512])],
                        sem="dG", dma=True)
            tm = p.step("tensor", [tL, tprev],
                        lambda e: e.matmul(psA[:, 0:512], RS0B, GBM[:, :],
                                           start=True, stop=True),
                        sem="cT")
            tprev = p.step("vector", [tm],
                           lambda e, i=i, dst=dst: e.tensor_copy(
                               out=dst[:, i * 512:(i + 1) * 512],
                               in_=psA[:, 0:512]),
                           sem="cV")
    tBC = tprev

    st = dict(last_out=None, last_tT=None, last_evY=None,
              prev_tRa=None, prev_trans=None, psB_free=None,
              resid_last=None, x1c_free=None, y2_free=[None] * 4,
              r_toks={}, mm_last={}, wload_prev=None, last_ffn=None,
              stat_tok=[None] * 4, nmu_free=[None] * 4)

    def emit_stats(grp):
        for sbq in range(4):
            sb = grp * 4 + sbq
            w1_ = [tBC, tCB]
            if st["nmu_free"][sbq] is not None:
                w1_.append(st["nmu_free"][sbq])
            t1 = p.step("vector", w1_,
                        lambda e, sb=sb, sbq=sbq: [
                            e.tensor_scalar(out=NMU4[sbq][:, :],
                                            in0=SG[:, sb:sb + 1],
                                            scalar1=-1.0 / FF, scalar2=None,
                                            op0=ALU.mult),
                            e.tensor_scalar(out=st4["e2"][:, :],
                                            in0=SG2[:, sb:sb + 1],
                                            scalar1=1.0 / FF, scalar2=None,
                                            op0=ALU.mult)][-1], sem="cV")
            t2_ = p.step("vector", [t1],
                         lambda e, sbq=sbq: e.tensor_tensor(
                             out=st4["v2"][:, :], in0=NMU4[sbq][:, :],
                             in1=NMU4[sbq][:, :], op=ALU.mult), sem="cV")
            t3 = p.step("vector", [t2_],
                        lambda e: e.tensor_tensor(out=st4["var"][:, :],
                                                  in0=st4["e2"][:, :],
                                                  in1=st4["v2"][:, :],
                                                  op=ALU.subtract), sem="cV")
            t4 = p.step("scalar", [t3],
                        lambda e: e.activation(out=st4["v2"][:, :],
                                               in_=st4["var"][:, :],
                                               func=AF.Sqrt,
                                               bias=EPS5b[:, :]), sem="cA")
            t5 = p.step("vector", [t4],
                        lambda e, sbq=sbq: e.reciprocal(
                            out=INV4[sbq][:, :], in_=st4["v2"][:, :]),
                        sem="cV")
            st["stat_tok"][sbq] = t5

    def emit_quant(grp, sbq):
        sb = grp * 4 + sbq
        t5 = st["stat_tok"][sbq]
        wg = [t5] + ([st["prev_tRa"]] if st["prev_tRa"] else [])
        def g_ld4(e, sb=sb):
            return [e.dma_start(
                out=U2F[:, q * DM:(q + 1) * DM],
                in_=io["gs"][sb * 128:(sb + 1) * 128, q * DM:(q + 1) * DM])
                for q in range(4)]
        g_ld4.ndma = 4
        g_tok = p.step("gpsimd", wg, g_ld4, sem="dG", dma=True)
        uq_tok = None
        for q in range(4):
            tq = p.step("vector", [g_tok] + ([uq_tok] if uq_tok else []),
                        lambda e, q=q, sbq=sbq: e.scalar_tensor_tensor(
                            out=U2F[:, q * DM:(q + 1) * DM],
                            in0=U2F[:, q * DM:(q + 1) * DM],
                            scalar=NMU4[sbq][:, :],
                            in1=GBC[:, q * DM:(q + 1) * DM],
                            op0=ALU.add, op1=ALU.mult), sem="cV")
            tu1 = p.step("vector", [tq],
                         lambda e, q=q, sbq=sbq: e.tensor_scalar(
                             out=U2F[:, q * DM:(q + 1) * DM],
                             in0=U2F[:, q * DM:(q + 1) * DM],
                             scalar1=INV4[sbq][:, :], scalar2=None,
                             op0=ALU.mult), sem="cV")
            uq_tok = p.step("vector", [tu1],
                            lambda e, q=q: e.tensor_tensor(
                                out=U2F[:, q * DM:(q + 1) * DM],
                                in0=U2F[:, q * DM:(q + 1) * DM],
                                in1=BBC[:, q * DM:(q + 1) * DM], op=ALU.add),
                            sem="cV")
        if sb == io.get("dbgsb", 0) and io.get("dbg5") is not None:
            dma_step(p, "gpsimd", "dG", [uq_tok], [(io["dbg5"][:, :], U2F[:, :])])
        tMx = p.step("vector", [uq_tok],
                     lambda e: e.reduce_max(out=st4["m2"][:, :], in_=U2F[:, :],
                                            axis=AX.X,
                                            apply_absolute_value=True),
                     sem="cV")
        tMc = p.step("vector", [tMx],
                     lambda e, sb=sb: e.tensor_scalar(
                         out=M2C[:, sb:sb + 1], in0=st4["m2"][:, :],
                         scalar1=1e-5, scalar2=None, op0=ALU.max), sem="cV")
        tC2 = p.step("vector", [tMc],
                     lambda e, sb=sb: e.tensor_scalar(
                         out=C2S[:, sb:sb + 1], in0=M2C[:, sb:sb + 1],
                         scalar1=CB2[:, :], scalar2=None, op0=ALU.mult),
                     sem="cV")
        tQ1 = p.step("vector", [tC2],
                     lambda e, sb=sb: e.reciprocal(
                         out=st4["v2"][:, :], in_=M2C[:, sb:sb + 1]),
                     sem="cV")
        tQs = p.step("vector", [tQ1],
                     lambda e: e.tensor_scalar(out=st4["qs"][:, :],
                                               in0=st4["v2"][:, :],
                                               scalar1=127.0, scalar2=None,
                                               op0=ALU.mult), sem="cV")
        tZ0 = p.step("vector", [tQs],
                     lambda e: ts2(e, U2F[:, :], U2F[:, :], st4["qs"][:, :],
                                   127.0, ALU.mult, ALU.add), sem="cV")
        wzz = [tZ0] + ([st["prev_trans"]] if st["prev_trans"] else [])
        tZZ = p.step("vector", wzz,
                     lambda e: e.tensor_copy(out=ZZF[:, :], in_=U2F[:, :]),
                     sem="cV")
        tLad = tZZ
        for q in range(8):
            tLad = ladder(p, ZZF[:, q * 1024:(q + 1) * 1024], GBUF[:, :],
                          1024, [tLad], f"lad2_{sb}_{q}")
        tRa = p.step("vector", [tLad],
                     lambda e: e.scalar_tensor_tensor(
                         out=ZZF[:, :], in0=ZZF[:, :], scalar=-1.0,
                         in1=U2F[:, :], op0=ALU.mult, op1=ALU.add),
                     sem="cV")
        tR = p.step("vector", [tRa],
                    lambda e: e.tensor_scalar(out=ZZF[:, :], in0=ZZF[:, :],
                                              scalar1=-127.0, scalar2=None,
                                              op0=ALU.add), sem="cV")
        st["prev_tRa"] = tRa
        if sb == io.get("dbgsb", 0) and io.get("dbg6") is not None:
            dma_step(p, "gpsimd", "dG", [tR], [(io["dbg6"][:, :], ZZF[:, :])])
        ev_tok = None
        for bi in range(16):
            def trr(e, bi=bi):
                last = None
                for j in range(4):
                    blk = bi * 4 + j
                    last = e.transpose(psB[:, j * 512:j * 512 + 128],
                                       ZZF[:, blk * 128:(blk + 1) * 128],
                                       EYE[:, :])
                return last
            wts = [tR] + ([ev_tok] if ev_tok else [])
            if st["last_ffn"] is not None and bi == 0 and sbq == 0:
                wts.append(st["last_ffn"])
            if st["psB_free"] is not None and bi == 0:
                wts.append(st["psB_free"])
            tT = p.step("tensor", wts, trr, sem="cT")

            def evr(e, bi=bi, sbq=sbq):
                last = None
                for j in range(4):
                    kc = bi * 4 + j
                    last = e.tensor_copy(
                        out=RQ2T[:, (sbq * 64 + kc) * 128:
                                 (sbq * 64 + kc + 1) * 128],
                        in_=psB[:, j * 512:j * 512 + 128])
                return last
            ev_tok = p.step("vector", [tT], evr, sem="cV")
        st["prev_trans"] = ev_tok
        st["r_toks"].setdefault(grp, []).append(ev_tok)

    def emit_mms(grp, eq):
        mm_toks = []
        for kc in range(64):
            wbuf = W2S[kc % 4]
            w = []
            if kc >= 4:
                w = [mm_toks[kc - 4]]
            else:
                if st["wload_prev"] is not None:
                    w.append(st["wload_prev"][kc])
                if st["resid_last"] is not None and kc < 2:
                    w.append(st["resid_last"])

            def ldp(e, kc=kc, eq=eq, wbuf=wbuf):
                return [e.dma_start(
                    out=wbuf[:, :],
                    in_=io["wq2t"][kc * 128:(kc + 1) * 128,
                                   eq * 512:(eq + 1) * 512])]
            ldp.ndma = 1
            tWp = p.step("sync" if kc % 2 == 0 else "gpsimd", w, ldp,
                         sem="dS" if kc % 2 == 0 else "dG", dma=True)

            def mmf(e, kc=kc, eq=eq, wbuf=wbuf):
                last = None
                for tb in range(4):
                    last = e.matmul(
                        psA[:, tb * 512:(tb + 1) * 512],
                        RQ2T[:, (tb * 64 + kc) * 128:(tb * 64 + kc + 1) * 128],
                        wbuf[:, :], start=(kc == 0), stop=(kc == 63))
                return last
            wm = [tWp]
            if kc == 0:
                wm += list(st["r_toks"][grp])
                if st["last_evY"] is not None:
                    wm.append(st["last_evY"])
            tMf = p.step("tensor", wm, mmf, sem="cT")
            mm_toks.append(tMf)
        st["mm_last"][(grp, eq)] = mm_toks[-1]
        st["last_ffn"] = mm_toks[-1]
        st["wload_prev"] = [mm_toks[60], mm_toks[61], mm_toks[62], mm_toks[63]]
        for tb in range(4):
            sb = grp * 4 + tb
            wev = [mm_toks[-1]] + \
                ([st["y2_free"][tb]] if st["y2_free"][tb] else [])
            tEy = p.step("scalar", wev,
                         lambda e, tb=tb, sb=sb, eq=eq: e.activation(
                             out=Y2[tb][:, eq * 512:(eq + 1) * 512],
                             in_=psA[:, tb * 512:(tb + 1) * 512],
                             func=AF.Identity,
                             scale=C2S[:, sb:sb + 1]), sem="cA")
            st["last_evY"] = tEy
            if (grp == io.get("dbgsb", 0) // 4 and tb == io.get("dbgsb", 0) % 4
                    and eq == 3 and io.get("dbg7") is not None):
                dma_step(p, "gpsimd", "dG", [tEy],
                         [(io["dbg7"][:, :], Y2[tb][:, :])])

    def emit_ln(grp):
        lnw = [st["last_evY"]] + \
            ([st["prev_trans"]] if st["prev_trans"] else [])
        resid_last = None
        for tb in range(4):
            sb = grp * 4 + tb
            slot = tb
            tB2 = p.step("vector", lnw,
                         lambda e, slot=slot: e.tensor_tensor(
                             out=Y2[slot][:, :], in0=Y2[slot][:, :],
                             in1=B2BC[:, :], op=ALU.add), sem="cV")
            lnw = [tB2]
            tDmp, tIy = ln_statsP(p, Y2[slot][:, :], DM, psB[:, 0:DM],
                                  EPS5b[:, :], st4["nmu"][:, :],
                                  st4["inv"][:, :], st4x(st4),
                                  lnw, "lny2")
            tN = p.step("vector", [tIy],
                        lambda e, slot=slot: ts2(e, Y2[slot][:, :],
                                                 Y2[slot][:, :],
                                                 st4["nmu"][:, :],
                                                 st4["inv"][:, :],
                                                 ALU.add, ALU.mult), sem="cV")
            tAdd = tN
            for c in range(4):
                wx = [st["mm_last"][(grp, 3)]]
                if st["x1c_free"]:
                    wx.append(st["x1c_free"])
                tXc = p.step("gpsimd", wx,
                             lambda e, sb=sb, c=c: [e.dma_start(
                                 out=X1C[:, :],
                                 in_=io["x1s"][sb * 128:(sb + 1) * 128,
                                               c * 512:(c + 1) * 512])],
                             sem="dG", dma=True)
                tAdd = p.step("vector", [tXc, tN] + ([tAdd] if c > 0 else []),
                              lambda e, slot=slot, c=c: e.tensor_tensor(
                                  out=Y2[slot][:, c * 512:(c + 1) * 512],
                                  in0=X1C[:, :],
                                  in1=Y2[slot][:, c * 512:(c + 1) * 512],
                                  op=ALU.add), sem="cV")
                st["x1c_free"] = tAdd
            resid_last = tAdd
            lnw = [tAdd]
            outv = io["out"][:, :].rearrange("(b r s) d -> b s r d", b=2, s=16)
            tSt = p.step("scalar",
                         [tAdd] + ([st["last_out"]] if st["last_out"] else []),
                         lambda e, sb=sb, slot=slot, outv=outv: [e.dma_start(
                             out=outv[sb // 16, sb % 16],
                             in_=Y2[slot][:, :])], sem="dA", dma=True)
            st["last_out"] = tSt
            st["y2_free"][slot] = tSt
        st["psB_free"] = resid_last
        st["resid_last"] = resid_last

    for grp in range(NG):
        emit_stats(grp)
        if grp >= 1:
            for eq in range(4):
                emit_mms(grp - 1, eq)
        for sbq in range(4):
            emit_quant(grp, sbq)
        if grp >= 1:
            emit_ln(grp - 1)
    for eq in range(4):
        emit_mms(NG - 1, eq)
    emit_ln(NG - 1)
    return dict(done=[st["last_out"], p.tok("cV"), p.tok("cT"), p.tok("cA")])


def st4x(st4):
    return {"ssum": st4["s1"], "ssq": st4["s2"], "v1": st4["var"],
            "v2": st4["v2"]}



def build_full(nc, io):
    from contextlib import ExitStack
    ctx = ExitStack()
    p = Prog(nc, ctx)
    sb = lambda n, s, d=F32: ctx.enter_context(nc.sbuf_tensor(n, s, d))

    RS0F = sb("ps_rs0f", [16, 128], F32)
    PEYE = sb("ps_eye", [128, 128], F32)
    PRS0B = sb("ps_rs0b", [16, 128], BF16)
    SG = sb("ps_sg", [128, NSB], F32)
    SG2 = sb("ps_sg2", [128, NSB], F32)
    CSx = sb("ps_cs", [128, NSB], F32)
    CBx = sb("ps_cb", [128, NSB], F32)
    M2C = sb("ps_m2c", [128, NSB], F32)
    C2S = sb("ps_c2s", [128, NSB], F32)
    THR = [sb("ps_thr1", [128, 1], F32), sb("ps_thr2", [128, 1], F32)]
    MAB = [sb("ps_ma1", [128, 1], F32), sb("ps_ma2", [128, 1], F32)]

    tPs = dma_step(p, "sync", "dS", [],
                   [(RS0F[:, :], io["rs0f"][:, :]),
                    (PEYE[:, :], io["eye"][:, :]),
                    (PRS0B[:, :], io["rs1"][0:16, 0:128])])

    b1 = p1_build(p, io)
    set_eye(p, b1["EYE"])
    p1_tiles(p, b1, io)
    p.barrier()

    ov2 = Overlay(nc, b1["WQO"])
    set_eye(p, PEYE)
    b2 = p2_build(p, io, ov2.mk, b1["psA"], THR, MAB, RS0F, [tPs, p.tok("cV")])
    p.barrier()

    ov3 = Overlay(nc, b1["WQO"])
    RS0B = PRS0B[:, :]
    b3 = p3_build(p, io, ov3.mk, b1["psA"], MAB[0], RS0B, SG, SG2, CSx, CBx,
                  [p.tok("cV")])
    p.barrier()

    ov4 = Overlay(nc, b1["WQO"])
    b4 = p4_build(p, io, ov4.mk, b1["psA"], b1["psB"], PEYE, MAB[1], RS0B,
                  SG, SG2, M2C, C2S, [p.tok("cV")])
    p.wait_only("gpsimd", [p.tok(s) for s in list(p.sems)])
    p.finalize()
    ctx.close()
    return nc




def _build_program():
    nc = bass.Bass("TRN2", target_bir_lowering=False,
                   detect_race_conditions=False)
    dr = lambda n, s, d=F32, k="ExternalInput": nc.dram_tensor(
        n, s, d, kind=k).ap()
    io = dict(
        x=dr("x", [NTOK, DM]),
        Wqkv=dr("Wqkv", [DM, 3 * DM]),
        Wo=dr("Wo", [DM, DM]),
        Ws=dr("Ws", [DM, 16]),
        bs_c=dr("bs_c", [16, 1]),
        lng_c=dr("lng_c", [128, 1]),
        lnb_c=dr("lnb_c", [128, 1]),
        w0=dr("w0", [16, 128, 128]),
        w1=dr("w1", [16, 128, 128]),
        w2=dr("w2", [16, 128, 128]),
        eye=dr("eye", [128, 128]),
        rs1=dr("rs1", [16, 16 * 128], BF16),
        rsb4=dr("rsb4", [4, 512], BF16),
        rs0f=dr("rs0f", [16, 128]),
        f_w1=dr("f_w1", [FF, DM]),
        f_w2=dr("f_w2", [DM, FF]),
        b1r=dr("b1r", [1, FF]),
        gmr=dr("gmr", [1, FF]),
        ber=dr("ber", [1, FF]),
        b2r=dr("b2r", [1, DM]),
        out=dr("out", [NTOK, DM], F32, "ExternalOutput"),
        x1s=dr("x1s", [NTOK, DM], F32, "Internal"),
        rqTs=dr("rqTs", [DM, NTOK], BF16, "Internal"),
        cs1=dr("cs1", [128, NSB], F32, "Internal"),
        gs=dr("gs", [NTOK, FF], BF16, "Internal"),
        wq1t=dr("wq1t", [DM, FF], BF16, "Internal"),
        wq2t=dr("wq2t", [FF, DM], BF16, "Internal"),
    )
    build_full(nc, io)
    return nc


def kernel(**inputs):
    import ml_dtypes
    from concourse.bass_utils import run_bass_kernel_spmd
    f32 = np.float32
    bf = ml_dtypes.bfloat16
    NC = 8
    x = np.asarray(inputs["L0_attn_x"], dtype=f32).reshape(-1, DM)
    common = dict(
        Wqkv=np.asarray(inputs["L0_attn_W_qkv"], f32),
        Wo=np.asarray(inputs["L0_attn_W_o"], f32),
        Ws=np.asarray(inputs["L0_attn_W_s"], f32),
        bs_c=np.asarray(inputs["L0_attn_b_s"], f32).reshape(16, 1),
        lng_c=np.asarray(inputs["L0_attn_ln_g"], f32).reshape(128, 1),
        lnb_c=np.asarray(inputs["L0_attn_ln_b"], f32).reshape(128, 1),
        w0=np.asarray(inputs["L0_attn_w0"], f32),
        w1=np.asarray(inputs["L0_attn_w1"], f32),
        w2=np.asarray(inputs["L0_attn_w2"], f32),
        eye=np.eye(128, dtype=f32),
        rs1=np.kron(np.eye(16, dtype=f32), np.ones((1, 128), f32)).astype(bf),
        rsb4=np.kron(np.eye(4, dtype=f32), np.ones((1, 128), f32)).astype(bf),
        rs0f=np.concatenate([np.ones((1, 128), f32),
                             np.zeros((15, 128), f32)]),
        f_w1=np.asarray(inputs["L0_ffn_w1"], f32),
        f_w2=np.asarray(inputs["L0_ffn_w2"], f32),
        b1r=np.asarray(inputs["L0_ffn_b1"], f32).reshape(1, FF),
        gmr=np.asarray(inputs["L0_ffn_gamma"], f32).reshape(1, FF),
        ber=np.asarray(inputs["L0_ffn_beta"], f32).reshape(1, FF),
        b2r=np.asarray(inputs["L0_ffn_b2"], f32).reshape(1, DM),
    )
    nc = _build_program()
    ims = []
    for c in range(NC):
        im = dict(common)
        im["x"] = np.ascontiguousarray(x[c * NTOK:(c + 1) * NTOK])
        ims.append(im)
    res = run_bass_kernel_spmd(nc, ims, core_ids=list(range(NC)))
    out = np.concatenate(
        [np.asarray(res.results[c]["out"], dtype=np.float32)
         for c in range(NC)], axis=0)
    return out.reshape(16, 2048, DM)
